# version 18
# baseline (speedup 1.0000x reference)
import math
from contextlib import ExitStack

import numpy as np
import ml_dtypes

import concourse.bass as bass
import concourse.mybir as mybir
from concourse.bass_utils import run_bass_kernel_spmd

F32 = mybir.dt.float32
BF16 = mybir.dt.bfloat16
AF = mybir.ActivationFunctionType
ALU = mybir.AluOpType
AX = mybir.AxisListType.X
NBF = ml_dtypes.bfloat16

T = 2048
D = 1024
HFF = 4096
HEADS = 16
NT = 16
NDC = 8
NHC = 32
NCH = 4
MAGIC = 12582912.0
CEXP = 12.0

ENGS = ("tensor", "vector", "scalar", "gpsimd", "sync")


class _Slot:

    def __init__(self, name):
        self.name = name
        self.count = 0
        self.sem = None


class _Op:
    __slots__ = ("eng", "fn", "waits", "amt", "marked", "val", "slot", "small")

    def __init__(self, eng, fn, waits, amt, slot, small=False):
        self.eng, self.fn, self.waits, self.amt = eng, fn, waits, amt
        self.marked = False
        self.val = None
        self.slot = slot
        self.small = small


class Prog:

    def __init__(self, nc):
        self.nc = nc
        self.q = {e: [] for e in ENGS}
        self.last = {e: None for e in ENGS}
        self.slots = []
        self.slot_last = {}

    def slot(self, name):
        s = _Slot(name)
        self.slots.append(s)
        return s

    def op(self, eng, fn, waits=(), dma=None, small=False):
        ws = []
        for w in waits:
            if w is None:
                continue
            assert isinstance(w, _Op), f"bad wait {w!r}"
            if w.eng == eng and w.slot is None:
                continue
            if w not in ws:
                ws.append(w)
        o = _Op(eng, fn, ws, 16 if dma is not None else 1, dma, small=small)
        if dma is not None:
            o.marked = True
            dma.count += 16
            o.val = dma.count
            self.slot_last[id(dma)] = o
        self.q[eng].append(o)
        self.last[eng] = o
        return o

    def barrier(self):
        lasts = {e: self.last[e] for e in ENGS if self.last[e] is not None}
        slot_lasts = list(self.slot_last.values())
        for e in ENGS:
            ws = [o for e2, o in lasts.items() if e2 != e and o.slot is None]
            ws += slot_lasts
            if ws:
                self.op(e, None, waits=ws)

    CHAIN = ()

    def emit(self, stack):
        nc = self.nc
        for e in self.CHAIN:
            prev = None
            for o in self.q[e]:
                if o.slot is not None or o.fn is None:
                    continue
                if prev is not None and prev not in o.waits:
                    o.waits.append(prev)
                prev = o
        for e in ENGS:
            prev = None
            for o in self.q[e]:
                if o.slot is not None or o.fn is None:
                    continue
                if prev is not None and prev.small and prev not in o.waits:
                    o.waits.append(prev)
                prev = o
        for e in ENGS:
            for o in self.q[e]:
                for w in o.waits:
                    w.marked = True
        sems = {e: stack.enter_context(nc.semaphore(f"sem_{e}")) for e in ENGS}
        for s in self.slots:
            s.sem = stack.enter_context(nc.semaphore(f"slt_{s.name}"))
        for e in ENGS:
            c = 0
            for o in self.q[e]:
                if o.slot is None and o.marked:
                    c += o.amt
                    o.val = c
        spacer = stack.enter_context(nc.sbuf_tensor("spacer_junk", [128, 16], F32))
        block = stack.enter_context(nc.Block())

        def sem_of(o):
            return o.slot.sem if o.slot is not None else sems[o.eng]

        def key_of(o):
            return id(o.slot) if o.slot is not None else o.eng

        def mk(e):
            def body(eng):
                maxw = {}
                for o in self.q[e]:
                    need = {}
                    for w in o.waits:
                        k = key_of(w)
                        if maxw.get(k, -1) < w.val and need.get(k, -1) < w.val:
                            need[k] = w.val
                    for w in o.waits:
                        k = key_of(w)
                        if need.get(k, -1) == w.val:
                            eng.wait_ge(sem_of(w), w.val)
                            maxw[k] = w.val
                            del need[k]
                    if o.fn is None:
                        if o.marked and o.slot is None:
                            eng.nop().then_inc(sems[e], o.amt)
                        continue
                    inst = o.fn(eng)
                    if o.marked:
                        inst.then_inc(sem_of(o), o.amt)
            return body

        block.tensor(mk("tensor"))
        block.vector(mk("vector"))
        block.scalar(mk("scalar"))
        block.gpsimd(mk("gpsimd"))
        block.sync(mk("sync"))
        return {e: len(self.q[e]) for e in ENGS}


def _oh_const():
    n = np.arange(T)
    val = 16 + (np.log(np.maximum(n, 1).astype(np.float32) / np.float32(16.0))
                / np.float32(math.log(8.0)) * np.float32(16.0)).astype(np.int32)
    val = np.minimum(val, 31)
    buckets = np.where(n < 16, n, val)
    oh = np.zeros((33, 2560), np.float32)
    oh[buckets[::-1], np.arange(T)] = 1.0
    oh[31, :T] -= 1.0
    oh[32, T:] = 1.0
    return oh


NPHASES = 4
DEBUG = False


def build_program():
    nc = bass.Bass("TRN2", target_bir_lowering=False,
                   detect_race_conditions=False)
    stack = ExitStack()

    dx = nc.dram_tensor("x", [T, D], F32, kind="ExternalInput")
    dwq = nc.dram_tensor("wq", [D, D], F32, kind="ExternalInput")
    dwk = nc.dram_tensor("wk", [D, D], F32, kind="ExternalInput")
    dwv = nc.dram_tensor("wv", [D, D], F32, kind="ExternalInput")
    dwo = nc.dram_tensor("wo", [D, D], F32, kind="ExternalInput")
    dbo = nc.dram_tensor("bo", [D], F32, kind="ExternalInput")
    drel = nc.dram_tensor("rel", [32, HEADS], F32, kind="ExternalInput")
    dw1t = nc.dram_tensor("w1t", [D, HFF], F32, kind="ExternalInput")
    dw2t = nc.dram_tensor("w2t", [HFF, D], F32, kind="ExternalInput")
    dcw = nc.dram_tensor("cw", [3, 128, NHC], F32, kind="ExternalInput")
    dcb = nc.dram_tensor("cb", [128, NHC], F32, kind="ExternalInput")
    dout = nc.dram_tensor("out", [T, D], F32, kind="ExternalOutput")

    dX1 = nc.dram_tensor("X1", [T, D], F32)
    dX2 = nc.dram_tensor("X2", [T, D], F32)
    dWQ1 = nc.dram_tensor("WQ1", [NDC, 128, HFF], BF16)
    dWQ2 = nc.dram_tensor("WQ2", [NHC, 128, D], BF16)
    dDB = nc.dram_tensor("DB", [HEADS, 2560], BF16)
    dSX = nc.dram_tensor("SX", [2052], F32)
    dS2 = nc.dram_tensor("S2", [T], F32)
    dRD = nc.dram_tensor("RD", [32, 1024], F32)

    identb = nc.inline_tensor(np.eye(128, dtype=NBF), name="identb")
    identf = nc.inline_tensor(np.eye(128, dtype=np.float32), name="identf")
    onescol = nc.inline_tensor(np.ones((128, 1), np.float32), name="onescol")
    onesrow = nc.inline_tensor(np.ones((1, 128), np.float32), name="onesrow")
    doh = nc.inline_tensor(_oh_const(), name="ohc")

    P = Prog(nc)
    SB = lambda name, shape, dt_, st=stack: st.enter_context(
        nc.sbuf_tensor(name, shape, dt_))
    PS = lambda name, shape, dt_, st=stack: st.enter_context(
        nc.psum_tensor(name, shape, dt_))

    IDB = SB("idb", [128, 128], BF16)
    IDF = SB("idf", [128, 128], F32)
    ONC = SB("onc", [128, 1], F32)
    ONR = SB("onr", [1, 128], F32)
    TAP = [SB(f"tap{k}", [128, NHC], F32) for k in range(3)]
    CBT = SB("cbt", [128, NHC], F32)
    BC12 = SB("bc12", [128, 1], F32)
    S125 = SB("s125", [128, 1], F32)
    SW1I = SB("sw1i", [128, 1], F32)
    SW2I = SB("sw2i", [128, 1], F32)
    SW1V = SB("sw1v", [128, 1], F32)
    SW2V = SB("sw2v", [128, 1], F32)
    PB = [PS(f"pbank{i}", [128, 512], F32) for i in range(8)]

    sl_const = P.slot("consts")
    P.op("sync", lambda e: e.dma_start(out=IDB[:], in_=identb.ap()[:, :]), dma=sl_const)
    P.op("sync", lambda e: e.dma_start(out=IDF[:], in_=identf.ap()[:, :]), dma=sl_const)
    P.op("sync", lambda e: e.dma_start(out=ONC[:], in_=onescol.ap()[:, :]), dma=sl_const)
    P.op("sync", lambda e: e.dma_start(out=ONR[:], in_=onesrow.ap()[:, :]), dma=sl_const)
    P.op("sync", lambda e: e.dma_start(out=CBT[:], in_=dcb.ap()[:, :]), dma=sl_const)
    ld_all = None
    for k in range(3):
        ld_all = P.op("sync", lambda e, k=k: e.dma_start(
            out=TAP[k][:], in_=dcw.ap()[k, :, :]), dma=sl_const)
    ld_idb = ld_idf = ld_onc = ld_onr = ld_cbt = ld_all
    ld_tap = [ld_all, ld_all, ld_all]
    P.op("vector", lambda e: e.memset(BC12[:], -CEXP))
    P.op("vector", lambda e: e.memset(S125[:], 0.125))

    def emit_wquant(st, src, n_chunks, width, ddst, swv, swiv, extra, nm):
        wbuf = [SB(f"{nm}w{i}", [128, width], F32, st) for i in range(2)]
        qbuf = [SB(f"{nm}q{i}", [128, width], BF16, st) for i in range(2)]
        jabs = SB(f"{nm}ja", [128, width], F32, st)
        jrnd = SB(f"{nm}jr", [128, width], F32, st)
        acc = SB(f"{nm}ac", [128, n_chunks], F32, st)
        sums = SB(f"{nm}su", [128, 1], F32, st)
        sc = SB(f"{nm}sc", [1, 4], F32, st)
        sl_w = [P.slot(f"{nm}l{i}") for i in range(2)]
        sl_sp = [P.slot(f"{nm}sp{i}") for i in range(2)]
        absops = []
        for c in range(n_chunks):
            b = c % 2
            war = [absops[c - 2]] if c >= 2 else []
            ld = P.op("sync", lambda e, b=b, c=c: e.dma_start(
                out=wbuf[b][:], in_=src.ap()[c * 128:(c + 1) * 128, :]),
                waits=war, dma=sl_w[b])
            absops.append(P.op("scalar", lambda e, b=b, c=c: e.activation(
                jabs[:], wbuf[b][:], AF.Abs, accum_out=acc[:, c:c + 1]),
                waits=[ld]))
        rs = P.op("vector", lambda e, n_chunks=n_chunks: e.reduce_sum(
            sums[:], acc[:, 0:n_chunks], axis=AX), waits=absops)
        mt = P.op("tensor", lambda e: e.matmul(PB[0][0:1, 0:1], ONC[:], sums[:],
                                               start=True, stop=True),
                  waits=[rs, ld_onc])
        v1 = P.op("vector", lambda e, n_chunks=n_chunks, width=width:
                  e.tensor_scalar(sc[:, 0:1], PB[0][0:1, 0:1],
                                  1.0 / (n_chunks * 128 * width), 1e-5,
                                  ALU.mult, ALU.max), waits=[mt], small=True)
        v2 = P.op("vector", lambda e: e.reciprocal(sc[:, 1:2], sc[:, 0:1]),
                  small=True)
        v3 = P.op("vector", lambda e, extra=extra: e.tensor_scalar(
            sc[:, 2:3], sc[:, 0:1], extra, None, ALU.mult), small=True)
        mb = P.op("tensor", lambda e: e.matmul(
            PB[1][:, 0:2], ONR[:], sc[:, 1:3], start=True, stop=True),
            waits=[v2, v3, ld_onr])
        b1 = P.op("vector", lambda e, swv=swv: e.tensor_copy(
            swv[:], PB[1][:, 0:1]), waits=[mb], small=True)
        b2 = P.op("vector", lambda e, swiv=swiv: e.tensor_copy(
            swiv[:], PB[1][:, 1:2]), waits=[mb], small=True)
        spills = []
        rounds = []
        for c in range(n_chunks):
            b = c % 2
            war = [rounds[c - 2]] if c >= 2 else [absops[-1]]
            ld = P.op("sync", lambda e, b=b, c=c: e.dma_start(
                out=wbuf[b][:], in_=src.ap()[c * 128:(c + 1) * 128, :]),
                waits=war, dma=sl_w[b])
            r1 = P.op("vector", lambda e, b=b: e.tensor_scalar(
                jrnd[:], wbuf[b][:], swv[:], None, ALU.mult), waits=[b1, ld])
            rounds.append(r1)
            P.op("vector", lambda e: e.tensor_scalar(
                jrnd[:], jrnd[:], MAGIC, MAGIC, ALU.add, ALU.subtract))
            r3 = P.op("vector", lambda e, b=b: e.tensor_scalar(
                qbuf[b][:], jrnd[:], 1.0, -1.0, ALU.min, ALU.max),
                waits=[spills[c - 2]] if c >= 2 else [])
            spills.append(P.op("gpsimd", lambda e, b=b, c=c: e.dma_start(
                out=ddst.ap()[c, :, :], in_=qbuf[b][:]), waits=[r3],
                dma=sl_sp[b]))
        return spills, b2

    wst = ExitStack()
    sp1, _ = emit_wquant(wst, dw1t, NDC, HFF, dWQ1, SW1V, SW1I, 1.0 / 127.0, "w1")
    sp2, _ = emit_wquant(wst, dw2t, NHC, D, dWQ2, SW2V, SW2I, 0.5 / 127.0, "w2")

    relx = SB("relx", [33, HEADS], F32, wst)
    ohs = SB("ohs", [33, 2560], F32, wst)
    dbs = SB("dbs", [HEADS, 2560], BF16, wst)
    sl_dbin = P.slot("dbin")
    sl_dbsp = P.slot("dbsp")
    r1 = P.op("sync", lambda e: e.dma_start(out=relx[0:32, :], in_=drel.ap()[:, :]), dma=sl_dbin)
    r2 = P.op("vector", lambda e: e.memset(relx[32:33, :], -30000.0))
    r3 = P.op("sync", lambda e: e.dma_start(out=ohs[:], in_=doh.ap()[:, :]), dma=sl_dbin)
    dbev = []
    for i in range(5):
        mm = P.op("tensor", lambda e, i=i: e.matmul(
            PB[2 + i][0:HEADS, :], relx[:], ohs[:, i * 512:(i + 1) * 512],
            start=True, stop=True), waits=[r1, r2, r3])
        dbev.append(P.op("scalar", lambda e, i=i: e.activation(
            dbs[:, i * 512:(i + 1) * 512], PB[2 + i][0:HEADS, :], AF.Exp),
            waits=[mm]))
    db_sp = P.op("gpsimd", lambda e: e.dma_start(out=dDB.ap()[:, :], in_=dbs[:]),
                 waits=dbev, dma=sl_dbsp)
    P.barrier()
    wst.close()

    def emit_ln(SBL, src_dram, quant, XQdst, Ydst, srow, zrow, sx_sp_dram, jnk,
                tag):
        xt = [SBL(f"xt{i}", [128, D], F32) for i in range(4)]
        yt = [SBL(f"yt{i}", [128, D], F32) for i in range(2)] if quant else None
        qt = SBL("qt", [128, D], F32) if quant else None
        qb = [SBL(f"qb{i}", [128, D], BF16) for i in range(2)]
        stats = SBL("sta", [128, 16], F32)
        statq = SBL("stq", [128, 16], F32)
        sl_xt = [P.slot(f"{tag}xt{i}") for i in range(4)]
        sl_sx = P.slot(f"{tag}sxs") if quant else None
        prev_cons = {}
        prev_q1 = {}
        prev_tr = {}
        prev_d3 = None
        tr_list = []
        last_sx = None
        tr_flip = [0]
        zs = None
        zm = []
        if quant:
            z4 = P.op("vector", lambda e: e.memset(zrow[:], 0.0))
            z1 = P.op("gpsimd", lambda e: e.dma_start(
                out=sx_sp_dram.ap()[0:1], in_=zrow[0:1, 0:1]),
                waits=[z4], dma=sl_sx)
            zs = P.op("gpsimd", lambda e: e.dma_start(
                out=sx_sp_dram.ap()[2049:2052], in_=zrow[0:1, 0:3]),
                waits=[z4], dma=sl_sx)
            zm = [P.op("gpsimd", lambda e, d=d: e.memset(XQdst[d][:, 0:1], 0))
                  for d in range(NDC)]
            zm += [P.op("gpsimd", lambda e, d=d: e.memset(
                XQdst[d][:, 2049:2052], 0)) for d in range(NDC)]
        for tc in range(NCH):
            red, sq = [], []
            for j in range(4):
                tt = tc * 4 + j
                ld = P.op("sync", lambda e, j=j, tt=tt: e.dma_start(
                    out=xt[j][:], in_=src_dram.ap()[tt * 128:(tt + 1) * 128, :]),
                    waits=[prev_cons.get(j)], dma=sl_xt[j])
                red.append(P.op("vector", lambda e, j=j: e.reduce_sum(
                    stats[:, j:j + 1], xt[j][:], axis=AX),
                    waits=[ld, prev_d3], small=True))
                sq.append(P.op("scalar", lambda e, j=j: e.activation(
                    jnk[:], xt[j][:], AF.Square,
                    accum_out=stats[:, 4 + j:5 + j]), waits=[ld, prev_d3]))
            P.op("vector", lambda e: e.tensor_scalar(
                stats[:, 8:12], stats[:, 0:4], -1.0 / D, None, ALU.mult),
                waits=red, small=True)
            P.op("vector", lambda e: e.tensor_scalar(
                stats[:, 12:16], stats[:, 4:8], 1.0 / D, None, ALU.mult),
                waits=sq, small=True)
            P.op("vector", lambda e: e.tensor_tensor(
                statq[:, 0:4], stats[:, 8:12], stats[:, 8:12], ALU.mult),
                small=True)
            P.op("vector", lambda e: e.tensor_tensor(
                statq[:, 4:8], stats[:, 12:16], statq[:, 0:4], ALU.subtract),
                small=True)
            P.op("vector", lambda e: e.tensor_scalar(
                statq[:, 8:12], statq[:, 4:8], 1e-5, None, ALU.add),
                small=True)
            c5 = P.op("vector", lambda e: e.reciprocal(
                statq[:, 12:16], statq[:, 8:12]), small=True)
            c6 = P.op("scalar", lambda e: e.sqrt(stats[:, 0:4],
                                                 statq[:, 12:16]), waits=[c5])
            P.op("vector", lambda e: e.tensor_tensor(
                stats[:, 4:8], stats[:, 0:4], stats[:, 0:4], ALU.mult),
                waits=[c6], small=True)
            P.op("vector", lambda e: e.tensor_tensor(
                statq[:, 0:4], stats[:, 4:8], statq[:, 8:12], ALU.mult),
                small=True)
            P.op("vector", lambda e: e.tensor_scalar(
                statq[:, 4:8], statq[:, 0:4], -0.5, 1.5, ALU.mult, ALU.add),
                small=True)
            c10 = P.op("vector", lambda e: e.tensor_tensor(
                stats[:, 12:16], stats[:, 0:4], statq[:, 4:8], ALU.mult),
                small=True)
            c11 = P.op("vector", lambda e: e.tensor_tensor(
                statq[:, 8:12], stats[:, 8:12], stats[:, 12:16], ALU.mult),
                small=True)
            u2s = []
            for j in range(4):
                tt = tc * 4 + j
                b = tt % 2
                if quant:
                    ya = P.op("scalar", lambda e, b=b, j=j: e.activation(
                        yt[b][:], xt[j][:], AF.Identity, bias=statq[:, 8 + j:9 + j],
                        scale=stats[:, 12 + j:13 + j]),
                        waits=[c10, c11, prev_q1.get(b)])
                    mxo = P.op("vector", lambda e, b=b, j=j: e.reduce_max(
                        statq[:, 12 + j:13 + j], yt[b][:], axis=AX,
                        apply_absolute_value=True), waits=[ya], small=True)
                    u0 = P.op("vector", lambda e, j=j: e.tensor_scalar(
                        stats[:, 0 + j:1 + j], statq[:, 12 + j:13 + j],
                        1e-5, None, ALU.max), waits=[mxo], small=True)
                    u1 = P.op("vector", lambda e, j=j: e.reciprocal(
                        stats[:, 4 + j:5 + j], stats[:, 0 + j:1 + j]),
                        small=True)
                    u2 = P.op("vector", lambda e, j=j: e.tensor_scalar(
                        statq[:, 0 + j:1 + j], stats[:, 4 + j:5 + j],
                        127.0, None, ALU.mult), small=True)
                    u2s.append(u2)
                    q1 = P.op("vector", lambda e, b=b, j=j: e.tensor_scalar(
                        qt[:], yt[b][:], statq[:, 0 + j:1 + j], MAGIC,
                        ALU.mult, ALU.add), waits=[u2])
                    prev_q1[b] = q1
                    q2 = P.op("vector", lambda e, b=b: e.tensor_scalar(
                        qb[b][:], qt[:], MAGIC, None, ALU.subtract),
                        waits=[prev_tr.get(b)])
                    prev_cons[j] = q2
                    src_t = qb[b]
                else:
                    ya = P.op("scalar", lambda e, b=b, j=j: e.activation(
                        qb[b][:], xt[j][:], AF.Identity, bias=statq[:, 8 + j:9 + j],
                        scale=stats[:, 12 + j:13 + j]),
                        waits=[c10, c11, prev_tr.get(b)])
                    prev_cons[j] = ya
                    q2 = ya
                    src_t = qb[b]
                trs = None
                for dc in range(NDC):
                    f = tr_flip[0] % 2
                    pb_ap = PB[5 + f].ap()[:, 0:64].bitcast(BF16)
                    tr1 = P.op("tensor",
                               lambda e, dc=dc, pb_ap=pb_ap, src_t=src_t:
                               e.transpose(pb_ap,
                                           src_t[:, dc * 128:(dc + 1) * 128],
                                           IDB[:]),
                               waits=[q2, ld_idb]
                               + ([tr_list[-2]] if len(tr_list) >= 2 else []))
                    if quant:
                        tr2 = P.op("scalar",
                                   lambda e, dc=dc, tt=tt, pb_ap=pb_ap:
                                   e.activation(
                                       XQdst[dc][:, 1 + tt * 128:1 + (tt + 1) * 128],
                                       pb_ap, AF.Copy), waits=[tr1] + zm)
                    else:
                        tr2 = P.op("scalar",
                                   lambda e, dc=dc, tt=tt, pb_ap=pb_ap:
                                   e.activation(
                                       Ydst[dc][:, tt * 128:(tt + 1) * 128],
                                       pb_ap, AF.Copy), waits=[tr1])
                    tr_list.append(tr2)
                    trs = tr2
                    tr_flip[0] += 1
                prev_tr[b] = trs
            if quant:
                d3 = P.op("vector", lambda e: e.tensor_scalar(
                    stats[:, 4:8], stats[:, 0:4], SW1I[:], None, ALU.mult),
                    waits=u2s, small=True)
                trx = P.op("tensor", lambda e: e.transpose(
                    PB[7][0:4, 0:128], stats[:, 4:8], IDF[:]),
                    waits=[d3, ld_idf, last_sx])
                prev_d3 = trx
                cp4 = P.op("vector", lambda e: e.tensor_copy(
                    srow[:], PB[7][0:4, 0:128]), waits=[trx, last_sx])
                last_sx = P.op("gpsimd", lambda e, tc=tc: e.dma_start(
                    out=sx_sp_dram.ap()[1 + tc * 512:1 + (tc + 1) * 512],
                    in_=srow[:]), waits=[cp4, zs], dma=sl_sx)
        sx_sp = last_sx
        return tr_list[-1], sx_sp

    def emit_ffn(st, src_dram, dst_dram, tag):
        SBL = lambda n, sh, dt_: SB(f"{tag}{n}", sh, dt_, st)
        XQ = [SBL(f"xq{d}", [128, 2052], BF16) for d in range(NDC)]
        srow = SBL("srow", [4, 128], F32)
        zrow = SBL("zrow", [1, 4], F32)
        GB = [SBL(f"gb{h}", [128, 515], F32) for h in range(NHC)]
        CT = [SBL(f"ct{h}", [128, 3], F32) for h in range(NHC)]
        W1S = [[SBL(f"w1s{b}_{d}", [128, 512], BF16) for d in range(NDC)]
               for b in range(2)]
        W2S = [[SBL(f"w2s{b}_{j}", [128, D], BF16) for j in range(4)]
               for b in range(3)]
        sxb = [SBL(f"sxb{i}", [128, 512], F32) for i in range(2)]
        sxp = SBL("sxp", [128, 2], F32)
        s2b = SBL("s2b", [128, 512], F32)
        cva = SBL("cva", [128, 512], F32)
        cvb = SBL("cvb", [128, 512], F32)
        tqt = SBL("tq", [128, 512], F32)
        MX = SBL("mx", [128, 512], F32)
        MN = SBL("mn", [128, 512], F32)
        st2 = SBL("st2", [128, 16], F32)
        s2s4 = SBL("s2s4", [32, 128], F32)
        xr = [SBL(f"xr{i}", [128, D], F32) for i in range(2)]
        xo = [SBL(f"xo{i}", [128, D], F32) for i in range(2)]
        hqv = [GB[h].ap()[:, 0:256].bitcast(BF16) for h in range(NHC)]
        for h in range(NHC):
            assert tuple(hqv[h].shape) == (128, 512), hqv[h].shape

        ct0 = [P.op("vector", lambda e, h=h: e.memset(CT[h][:, 0:1], 0.0))
               for h in range(NHC)]
        s2z = P.op("vector", lambda e: e.memset(s2s4[:], 0.0))
        sl_sxb = P.slot(f"{tag}sxb")
        sl_sxp = P.slot(f"{tag}sxp")
        sl_w1 = [P.slot(f"{tag}w1{i}") for i in range(3)]
        sl_w2 = [P.slot(f"{tag}w2{i}") for i in range(3)]
        sl_s2s = P.slot(f"{tag}s2s")
        sl_s2b = P.slot(f"{tag}s2b")
        sl_xr = [P.slot(f"{tag}xr{i}") for i in range(2)]
        sl_xo = [P.slot(f"{tag}xo{i}") for i in range(2)]
        xq_done, sx_sp = emit_ln(SBL, src_dram, True, XQ, None, srow, zrow,
                                 dSX, xo[0], tag)

        geighth = [0]
        g2idx = [0]
        mm2_of_gen = {}
        last_mm1_of_eighth = {}
        prev_sxl = None
        prev_s2l = None
        prev_mm2ev = {k: None for k in range(8)}
        hq_ops = {}
        preev = []
        store_ops = []
        prev_trh = []
        prev_s2copy = None
        psxl = {}
        for c in range(NCH):
            sxl = P.op("gpsimd", lambda e, c=c: e.dma_start(
                out=sxb[c % 2][:],
                in_=bass.AP(dSX, 512 * c + 3, [[0, 128], [1, 512]])),
                waits=[sx_sp, psxl.get(c % 2)], dma=sl_sxb)
            if c == 0:
                sxpl = P.op("gpsimd", lambda e: e.dma_start(
                    out=sxp[:], in_=bass.AP(dSX, 1, [[0, 128], [1, 2]])),
                    waits=[sx_sp], dma=sl_sxp)
            mm1ev = {}
            mx_folds = []
            mn_folds = []
            mzero = P.op("vector", lambda e: e.memset(MX[:], 0.0),
                         waits=prev_trh)
            nzero = P.op("vector", lambda e: e.memset(MN[:], 0.0),
                         waits=prev_trh)
            conv_sil = []
            cur_lds = None
            cur_gb = 0
            for hc in range(NHC):
                eh = hc // 4
                if hc % 4 == 0:
                    g = geighth[0]
                    cur_gb = g % 2
                    war = [last_mm1_of_eighth.get(g - 2)]
                    cur_lds = [P.op("sync" if dn < 4 else "scalar",
                                    lambda e, gb=cur_gb, dn=dn, eh=eh:
                                    e.dma_start(
                                        out=W1S[gb][dn][:],
                                        in_=dWQ1.ap()[dn, :, eh * 512:(eh + 1) * 512]),
                                    waits=war + sp1[-2:],
                                    dma=sl_w1[cur_gb])
                               for dn in range(NDC)]
                    geighth[0] += 1
                bank = hc % 2
                if c == 0:
                    pmm = None
                    pwar = [preev[hc - 2]] if hc >= 2 else []
                    for dn in range(NDC):
                        pmm = P.op("tensor",
                                   lambda e, dn=dn, hc=hc, gb=cur_gb:
                                   e.matmul(
                                       PB[2 + hc % 2][:, 0:2],
                                       W1S[gb][dn][:, (hc % 4) * 128:(hc % 4 + 1) * 128],
                                       XQ[dn][:, 1:3],
                                       start=(dn == 0), stop=(dn == NDC - 1)),
                                   waits=cur_lds + [xq_done] + pwar)
                    preev.append(P.op("vector", lambda e, hc=hc: e.tensor_tensor(
                        CT[hc][:, 1:3],
                        PB[2 + hc % 2][:, 0:2],
                        sxp[:], ALU.mult), waits=[pmm, sxpl], small=True))
                mm = None
                for dn in range(NDC):
                    mm = P.op("tensor",
                              lambda e, dn=dn, hc=hc, bank=bank, c=c, gb=cur_gb:
                              e.matmul(
                                  PB[bank][:],
                                  W1S[gb][dn][:, (hc % 4) * 128:(hc % 4 + 1) * 128],
                                  XQ[dn][:, 512 * c + 3:512 * c + 515],
                                  start=(dn == 0), stop=(dn == NDC - 1)),
                              waits=cur_lds + [xq_done, mm1ev.get(hc - 2),
                                               prev_mm2ev[bank]])
                last_mm1_of_eighth[geighth[0] - 1] = mm
                war_gb = hq_ops.get((c - 1, hc))
                ev = P.op("vector", lambda e, hc=hc, bank=bank, c=c:
                          e.tensor_tensor(
                    GB[hc][:, 3:515], PB[bank][:], sxb[c % 2][:], ALU.mult),
                    waits=[mm, sxl, war_gb])
                mm1ev[hc] = ev
                cin = P.op("vector", lambda e, hc=hc: e.tensor_copy(
                    GB[hc][:, 0:3], CT[hc][:]),
                    waits=[ct0[hc], war_gb] + ([preev[hc]] if c == 0 else []),
                    small=True)
                cout = P.op("vector", lambda e, hc=hc: e.tensor_copy(
                    CT[hc][:], GB[hc][:, 512:515]), waits=[ev], small=True)
                cc = P.op("scalar", lambda e, hc=hc: e.activation(
                    cva[:], GB[hc][:, 1:513], AF.Identity,
                    bias=CBT[:, hc:hc + 1], scale=TAP[1][:, hc:hc + 1]),
                    waits=[ev, cin, ld_cbt, ld_tap[1]])
                s1 = P.op("vector", lambda e, hc=hc:
                          e.scalar_tensor_tensor(
                              cvb[:], GB[hc][:, 0:512], TAP[0][:, hc:hc + 1],
                              cva[:], ALU.mult, ALU.add),
                          waits=[cc, ld_tap[0]])
                s2_ = P.op("vector", lambda e, hc=hc:
                           e.scalar_tensor_tensor(
                               cva[:], GB[hc][:, 2:514], TAP[2][:, hc:hc + 1],
                               cvb[:], ALU.mult, ALU.add),
                           waits=[ld_tap[2]])
                sil = P.op("scalar", lambda e, hc=hc: e.activation(
                    GB[hc][:, 0:512], cva[:], AF.Silu), waits=[s2_, cout])
                conv_sil.append(sil)
                mx_folds.append(P.op("vector", lambda e, hc=hc: e.tensor_tensor(
                    MX[:], MX[:], GB[hc][:, 0:512], ALU.max),
                    waits=[sil, mzero]))
                mn_folds.append(P.op("vector", lambda e, hc=hc: e.tensor_tensor(
                    MN[:], MN[:], GB[hc][:, 0:512], ALU.min),
                    waits=[sil, nzero]))
            trh_reads = []
            for i in range(4):
                trh = P.op("tensor", lambda e, i=i: e.transpose(
                    PB[3][:, 0:128], MX[:, i * 128:(i + 1) * 128], IDF[:]),
                    waits=mx_folds + [ld_idf, prev_mm2ev[3]]
                    + ([trh_reads[-1]] if trh_reads else []))
                rd_ = P.op("vector", lambda e, i=i: e.reduce_max(
                    st2[:, 0 + i:1 + i], PB[3][:, 0:128], axis=AX), waits=[trh],
                    small=True)
                trn = P.op("tensor", lambda e, i=i: e.transpose(
                    PB[3][:, 0:128], MN[:, i * 128:(i + 1) * 128], IDF[:]),
                    waits=mn_folds + [rd_])
                rdn = P.op("vector", lambda e, i=i: e.tensor_reduce(
                    st2[:, 12 + i:13 + i], PB[3][:, 0:128], AX, ALU.min),
                    waits=[trn], small=True)
                trh_reads.append(rdn)
            negn = P.op("vector", lambda e: e.tensor_scalar(
                st2[:, 4:8], st2[:, 12:16], -1.0, None, ALU.mult),
                waits=trh_reads, small=True)
            pmax = P.op("vector", lambda e: e.tensor_tensor(
                st2[:, 8:12], st2[:, 0:4], st2[:, 4:8], ALU.max), small=True)
            prev_trh = trh_reads + [pmax]
            e0 = P.op("vector", lambda e: e.tensor_scalar(
                st2[:, 4:8], st2[:, 8:12], 1e-5, None, ALU.max), waits=[pmax],
                small=True)
            P.op("vector", lambda e: e.reciprocal(st2[:, 0:4], st2[:, 4:8]),
                 small=True)
            e2 = P.op("vector", lambda e: e.tensor_scalar(
                st2[:, 12:16], st2[:, 0:4], 127.0, None, ALU.mult), small=True)
            e3 = P.op("vector", lambda e: e.tensor_scalar(
                st2[:, 8:12], st2[:, 4:8], SW2I[:], None, ALU.mult), small=True)
            trs2 = P.op("tensor", lambda e: e.transpose(
                PB[7][0:4, 0:128], st2[:, 12:16], IDF[:]),
                waits=[e2, prev_s2copy, ld_idf, prev_mm2ev[7]])
            s2st = P.op("vector", lambda e: e.tensor_copy(
                s2s4[0:4, :], PB[7][0:4, 0:128]), waits=[trs2, s2z, prev_s2copy])
            s2shf = []
            for j in range(4):
                for dq in range(4):
                    s2shf.append(P.op("vector",
                        lambda e, j=j, dq=dq: e.stream_shuffle(
                            s2b[dq * 32:(dq + 1) * 32, j * 128:(j + 1) * 128],
                            s2s4[0:32, :], [j] * 32),
                        waits=[s2st, prev_s2l]))
            prev_s2copy = s2shf[-1]
            s2l = P.op("vector", lambda e: e.nop(), waits=s2shf)
            hq_list = []
            for hc in range(NHC):
                t1 = P.op("vector", lambda e, hc=hc: e.tensor_tensor(
                    tqt[:], GB[hc][:, 0:512], s2b[:], ALU.mult),
                    waits=[s2l, mx_folds[hc]])
                t2 = P.op("vector", lambda e, hc=hc: e.tensor_scalar(
                    hqv[hc], tqt[:], MAGIC, MAGIC, ALU.add, ALU.subtract))
                hq_ops[(c, hc)] = t2
                hq_list.append(t2)
            mm2 = {k: None for k in range(8)}
            for g in range(8):
                gg = g2idx[0]
                gb2 = gg % 3
                war = mm2_of_gen.get(gg - 3, [])
                l2 = [P.op("sync" if j < 2 else "scalar",
                           lambda e, gb2=gb2, j=j, g=g: e.dma_start(
                    out=W2S[gb2][j][:], in_=dWQ2.ap()[g * 4 + j, :, :]),
                    waits=war + sp2[-2:], dma=sl_w2[gb2]) for j in range(4)]
                mm2_of_gen[gg] = []
                g2idx[0] += 1
                for j in range(4):
                    hc = g * 4 + j
                    for tl in range(4):
                        for dh in range(2):
                            k = tl * 2 + dh
                            extra = []
                            if hc == 0:
                                extra.append(prev_mm2ev[k])
                                if k < 2:
                                    extra.append(mm1ev.get(NHC - 2 + k))
                            mm2[k] = P.op("tensor",
                                          lambda e, hc=hc, tl=tl, dh=dh, k=k,
                                          gb2=gb2, j=j:
                                          e.matmul(
                                              PB[k][:],
                                              hqv[hc][:, tl * 128:(tl + 1) * 128],
                                              W2S[gb2][j][:, dh * 512:(dh + 1) * 512],
                                              start=(hc == 0),
                                              stop=(hc == NHC - 1)),
                                          waits=[hq_list[hc]] + l2 + extra)
                            mm2_of_gen[gg].append(mm2[k])
            for tl in range(4):
                b = tl % 2
                tt = c * 4 + tl
                lxr = P.op("sync", lambda e, b=b, tt=tt: e.dma_start(
                    out=xr[b][:], in_=src_dram.ap()[tt * 128:(tt + 1) * 128, :]),
                    waits=[store_ops[-2]] if len(store_ops) >= 2 else [],
                    dma=sl_xr[b])
                evs = []
                for dh in range(2):
                    k = tl * 2 + dh
                    ev2 = P.op("vector", lambda e, k=k, b=b, dh=dh, tl=tl:
                               e.scalar_tensor_tensor(
                                   xo[b][:, dh * 512:(dh + 1) * 512], PB[k][:],
                                   st2[:, 8 + tl:9 + tl],
                                   xr[b][:, dh * 512:(dh + 1) * 512],
                                   ALU.mult, ALU.add),
                               waits=[mm2[k], lxr, e3,
                                      store_ops[-2] if len(store_ops) >= 2 else None])
                    prev_mm2ev[k] = ev2
                    evs.append(ev2)
                store_ops.append(P.op("gpsimd", lambda e, b=b, tt=tt:
                                      e.dma_start(
                                          out=dst_dram.ap()[tt * 128:(tt + 1) * 128, :],
                                          in_=xo[b][:]), waits=evs, dma=sl_xo[b]))
            psxl[c % 2] = mm1ev[NHC - 1]
            prev_s2l = hq_list[-1]
        return store_ops

    def emit_attn(st, src_dram, dst_dram, tag="a"):
        SBL = lambda n, sh, dt_: SB(f"{tag}{n}", sh, dt_, st)
        Y = [SBL(f"y{d}", [128, T], BF16) for d in range(NDC)]
        Q = [SBL(f"q{d}", [128, T], BF16) for d in range(NDC)]
        K = [SBL(f"k{d}", [128, T], BF16) for d in range(NDC)]
        VE = [SBL(f"ve{t}", [128, 1040], BF16) for t in range(NT)]
        WB = [SBL(f"wb{d}", [128, D], BF16) for d in range(NDC)]
        RA = [SBL(f"ra{i}", [128, 1024], BF16) for i in range(2)]
        RB = [SBL(f"rb{i}", [128, 1024], BF16) for i in range(2)]
        PAb = [SBL(f"pa{i}", [128, 512], BF16) for i in range(2)]
        PBb = [SBL(f"pt{i}", [128, 512], BF16) for i in range(2)]
        rdr = SBL("rdr", [128, 1024], F32)
        bob = rdr
        xr = [SBL(f"xr{i}", [128, D], F32) for i in range(2)]
        xo = [SBL(f"xo{i}", [128, D], F32) for i in range(2)]
        t1b = SBL("t1b", [128, 512], F32)
        AC = Y

        sl_bo = P.slot(f"{tag}bo")
        sl_wl = P.slot(f"{tag}wl")
        sl_ra = [P.slot(f"{tag}ra{i}") for i in range(2)]
        sl_rb = [P.slot(f"{tag}rbs{i}") for i in range(2)]
        sl_rd1 = P.slot(f"{tag}rd1")
        sl_rd2 = P.slot(f"{tag}rd2")
        sl_rd3 = P.slot(f"{tag}rd3")
        sl_xr = [P.slot(f"{tag}xr{i}") for i in range(2)]
        sl_xo = [P.slot(f"{tag}xo{i}") for i in range(2)]
        ld_bo = None
        rz = [P.op("vector", lambda e, tt=tt: e.memset(tt[64:96, :], 0.0))
              for tt in (rdr, xr[0])]
        vz = []
        for t in range(NT):
            for h2 in range(16):
                vz.append(P.op("gpsimd", lambda e, t=t, h2=h2: e.memset(
                    VE[t][:, h2 * 65 + 64:h2 * 65 + 65], 1.0)))
        vz_last = vz[-1]

        y_done, _ = emit_ln(SBL, src_dram, False, None, Y, None, None, None,
                            xo[0], tag)

        def load_w(dsrc, waits):
            return [P.op("gpsimd", lambda e, dc=dc, dsrc=dsrc: e.dma_start(
                out=WB[dc][:], in_=dsrc.ap()[dc * 128:(dc + 1) * 128, :]),
                waits=waits, dma=sl_wl) for dc in range(NDC)]

        qk_ev = []
        lw = load_w(dwq, [])
        for dc in range(NDC):
            for qc in range(NCH):
                bank = (dc * NCH + qc) % 2
                mm = None
                for dn in range(NDC):
                    mm = P.op("tensor", lambda e, dn=dn, dc=dc, qc=qc, bank=bank:
                              e.matmul(PB[bank][:],
                                       WB[dn][:, dc * 128:(dc + 1) * 128],
                                       Y[dn][:, qc * 512:(qc + 1) * 512],
                                       start=(dn == 0), stop=(dn == NDC - 1)),
                              waits=lw + [y_done]
                              + ([qk_ev[-2]] if len(qk_ev) >= 2 else []))
                qk_ev.append(P.op("scalar", lambda e, dc=dc, qc=qc, bank=bank:
                                  e.activation(Q[dc][:, qc * 512:(qc + 1) * 512],
                                               PB[bank][:], AF.Identity,
                                               scale=S125[:]), waits=[mm]))
        lw2 = load_w(dwk, [qk_ev[-1]])
        kv_ev = []
        for dc in range(NDC):
            for qc in range(NCH):
                bank = (dc * NCH + qc) % 2
                mm = None
                for dn in range(NDC):
                    mm = P.op("tensor", lambda e, dn=dn, dc=dc, qc=qc, bank=bank:
                              e.matmul(PB[bank][:],
                                       WB[dn][:, dc * 128:(dc + 1) * 128],
                                       Y[dn][:, qc * 512:(qc + 1) * 512],
                                       start=(dn == 0), stop=(dn == NDC - 1)),
                              waits=lw2
                              + ([kv_ev[-2]] if len(kv_ev) >= 2 else []))
                kv_ev.append(P.op("scalar", lambda e, dc=dc, qc=qc, bank=bank:
                                  e.activation(K[dc][:, qc * 512:(qc + 1) * 512],
                                               PB[bank][:], AF.Copy),
                                  waits=[mm]))
        lw3 = load_w(dwv, [kv_ev[-1]])
        v_ev = []
        for t in range(NT):
            for dhf in range(2):
                bank = (t * 2 + dhf) % 2
                mm = None
                for dn in range(NDC):
                    mm = P.op("tensor", lambda e, dn=dn, t=t, dhf=dhf, bank=bank:
                              e.matmul(PB[bank][:],
                                       Y[dn][:, t * 128:(t + 1) * 128],
                                       WB[dn][:, dhf * 512:(dhf + 1) * 512],
                                       start=(dn == 0), stop=(dn == NDC - 1)),
                              waits=lw3
                              + ([v_ev[-9]] if len(v_ev) >= 16 else []))
                for hl in range(8):
                    h = dhf * 8 + hl
                    slot = (h // 2) * 130 + (h % 2) * 65
                    v_ev.append(P.op("scalar",
                                     lambda e, t=t, hl=hl, slot=slot, bank=bank:
                                     e.activation(VE[t][:, slot:slot + 64],
                                                  PB[bank][:, hl * 64:(hl + 1) * 64],
                                                  AF.Copy),
                                     waits=[mm, vz_last]))
        v_done = v_ev[-1]

        norm_ops = []
        bank_rd = {}
        pab_rd = {}
        ra_rd = {0: [], 1: []}
        pv_rd = {}
        d1_hist = {}
        rb_rd = {}
        last_rload = None
        prev_d1 = None
        for pr in range(8):
            rp = pr % 2
            rlA = P.op("gpsimd", lambda e, pr=pr, rp=rp: e.dma_start(
                out=RA[rp][:], in_=bass.AP(dDB, (2 * pr) * 2560 + 1408,
                                           [[1, 128], [1, 1024]])),
                waits=[db_sp, last_rload] + ra_rd[rp], dma=sl_ra[rp])
            rlB = P.op("gpsimd", lambda e, pr=pr, rp=rp: e.dma_start(
                out=RB[rp][:], in_=bass.AP(dDB, (2 * pr + 1) * 2560 + 1408,
                                           [[1, 128], [1, 1024]])),
                waits=[db_sp, last_rload] + ra_rd[rp], dma=sl_rb[rp])
            last_rload = rlB
            ra_rd[rp] = []
            for qc in range(NCH):
                nkt = 4 * qc + 4
                p = qc % 2
                pvA = (2, 0)[p]
                pvB = pvA + 1
                oa_mm = None
                ob_mm = None
                for kt in range(nkt):
                    kb = kt % 2
                    near = kt >= 4 * qc - 1
                    j0 = 2047 - 512 * qc + 128 * kt
                    bA = 4 + 2 * kb
                    bB = bA + 1
                    we = bank_rd.get(kb, (None, None))
                    mA = P.op("tensor", lambda e, pr=pr, qc=qc, kt=kt, bA=bA:
                              e.matmul(PB[bA][:],
                                       K[pr][0:64, kt * 128:(kt + 1) * 128],
                                       Q[pr][0:64, qc * 512:(qc + 1) * 512],
                                       start=True, stop=True,
                                       tile_position=(0, 0)),
                              waits=[qk_ev[-1], kv_ev[-1], we[0]])
                    mB = P.op("tensor", lambda e, pr=pr, qc=qc, kt=kt, bB=bB:
                              e.matmul(PB[bB][:],
                                       K[pr][64:128, kt * 128:(kt + 1) * 128],
                                       Q[pr][64:128, qc * 512:(qc + 1) * 512],
                                       start=True, stop=True,
                                       tile_position=(64, 0)),
                              waits=[we[1]])
                    wpab = pab_rd.get(kb, (None, None))
                    eA = P.op("scalar", lambda e, kb=kb, bA=bA: e.activation(
                        PAb[kb][:], PB[bA][:], AF.Exp, bias=BC12[:]),
                        waits=[mA, wpab[0]])
                    eB = P.op("scalar", lambda e, kb=kb, bB=bB: e.activation(
                        PBb[kb][:], PB[bB][:], AF.Exp, bias=BC12[:]),
                        waits=[mB, wpab[1]])
                    bank_rd[kb] = (eA, eB)
                    if near:
                        pA = P.op("vector", lambda e, kb=kb, j0=j0, rp=rp:
                                  e.tensor_mul(
                                      PAb[kb][:], PAb[kb][:],
                                      bass.AP(RA[rp], j0 - 1408, [[1024, 128], [-1, 512]])),
                                  waits=[eA, rlA])
                        pB = P.op("vector", lambda e, kb=kb, j0=j0, rp=rp:
                                  e.tensor_mul(
                                      PBb[kb][:], PBb[kb][:],
                                      bass.AP(RB[rp], j0 - 1408, [[1024, 128], [-1, 512]])),
                                  waits=[eB, rlB])
                        ra_rd[rp] += [pA, pB]
                        srcA, srcB = pA, pB
                    else:
                        srcA, srcB = eA, eB
                    slotA = pr * 130
                    oa_mm = P.op("tensor", lambda e, kt=kt, slotA=slotA,
                                 nkt=nkt, kb=kb, pvA=pvA:
                                 e.matmul(PB[pvA][0:65, :],
                                          VE[kt][:, slotA:slotA + 65],
                                          PAb[kb][:],
                                          start=(kt == 0), stop=(kt == nkt - 1)),
                                 waits=[srcA, v_done]
                                 + (pv_rd.get(pvA, []) if kt == 0 else []))
                    ob_mm = P.op("tensor", lambda e, kt=kt, slotA=slotA,
                                 nkt=nkt, kb=kb, pvB=pvB:
                                 e.matmul(PB[pvB][0:65, :],
                                          VE[kt][:, slotA + 65:slotA + 130],
                                          PBb[kb][:],
                                          start=(kt == 0), stop=(kt == nkt - 1)),
                                 waits=[srcB]
                                 + (pv_rd.get(pvB, []) if kt == 0 else []))
                    pab_rd[kb] = (oa_mm, ob_mm)
                sidx = pr * 4 + qc
                RCP = (rdr, xr[0])[p]
                rc1 = P.op("vector", lambda e, p=p, pvA=pvA, RCP=RCP:
                           e.reciprocal(
                               RCP[64:65, 0:512], PB[pvA][64:65, :]),
                           waits=[oa_mm] + d1_hist.get(p, []), small=True)
                rc2 = P.op("vector", lambda e, p=p, pvB=pvB, RCP=RCP:
                           e.reciprocal(
                               RCP[64:65, 512:1024], PB[pvB][64:65, :]),
                           waits=[ob_mm], small=True)
                shf = []
                for half, c0 in ((0, 0), (1, 512)):
                    for dq in range(2):
                        shf.append(P.op("vector",
                            lambda e, p=p, dq=dq, c0=c0, RCP=RCP:
                            e.stream_shuffle(
                                xo[p][dq * 32:(dq + 1) * 32, c0:c0 + 512],
                                RCP[64:96, c0:c0 + 512], [0] * 32),
                            waits=[rc1 if c0 == 0 else rc2]
                            + rb_rd.get(p, [])))
                d1_hist[p] = shf
                nA = P.op("vector", lambda e, pr=pr, qc=qc, p=p, pvA=pvA:
                          e.tensor_mul(
                              AC[pr][0:64, qc * 512:(qc + 1) * 512],
                              PB[pvA][0:64, :],
                              xo[p][0:64, 0:512]),
                          waits=shf[0:2] + [oa_mm, v_done])
                nB = P.op("vector", lambda e, pr=pr, qc=qc, p=p, pvB=pvB:
                          e.tensor_mul(
                              AC[pr][64:128, qc * 512:(qc + 1) * 512],
                              PB[pvB][0:64, :],
                              xo[p][0:64, 512:1024]),
                          waits=shf[2:4] + [ob_mm])
                pv_rd[pvA] = [rc1, nA]
                pv_rd[pvB] = [rc2, nB]
                rb_rd[p] = [nA, nB]
                norm_ops += [nA, nB]
        ac_done = norm_ops[-1]

        ld_bo = P.op("gpsimd", lambda e: e.dma_start(
            out=bob[:], in_=bass.AP(dbo, 0, [[0, 128], [1, D]])),
            waits=[ac_done] + d1_hist.get(0, []) + d1_hist.get(1, []),
            dma=sl_bo)
        lw4 = load_w(dwo, [ac_done])
        store_ops = []
        wo_ev = []
        for t in range(NT):
            b = t % 2
            lxr = P.op("sync", lambda e, b=b, t=t: e.dma_start(
                out=xr[b][:], in_=src_dram.ap()[t * 128:(t + 1) * 128, :]),
                waits=([store_ops[-2]] if len(store_ops) >= 2 else
                       d1_hist.get(0, []) + d1_hist.get(1, [])),
                dma=sl_xr[b])
            evs = []
            for dhf in range(2):
                bank = (t * 2 + dhf) % 2
                mm = None
                for dn in range(NDC):
                    mm = P.op("tensor", lambda e, dn=dn, t=t, dhf=dhf, bank=bank:
                              e.matmul(PB[bank][:],
                                       AC[dn][:, t * 128:(t + 1) * 128],
                                       WB[dn][:, dhf * 512:(dhf + 1) * 512],
                                       start=(dn == 0), stop=(dn == NDC - 1)),
                              waits=lw4 + [ac_done]
                              + ([wo_ev[-2]] if len(wo_ev) >= 2 else []))
                tb = P.op("vector", lambda e, bank=bank, dhf=dhf: e.tensor_add(
                    t1b[:], PB[bank][:], bob[:, dhf * 512:(dhf + 1) * 512]),
                    waits=[mm, ld_bo])
                ev = P.op("vector", lambda e, b=b, dhf=dhf: e.tensor_add(
                    xo[b][:, dhf * 512:(dhf + 1) * 512], t1b[:],
                    xr[b][:, dhf * 512:(dhf + 1) * 512]),
                    waits=[tb, lxr,
                           store_ops[-2] if len(store_ops) >= 2 else None])
                wo_ev.append(ev)
                evs.append(ev)
            store_ops.append(P.op("gpsimd", lambda e, b=b, t=t: e.dma_start(
                out=dst_dram.ap()[t * 128:(t + 1) * 128, :], in_=xo[b][:]),
                waits=evs, dma=sl_xo[b]))
        return store_ops

    if NPHASES == 1:
        cp = [P.op("gpsimd", lambda e, t=t: e.dma_start(
            out=dout.ap()[t * 128:(t + 1) * 128, :],
            in_=dx.ap()[t * 128:(t + 1) * 128, :]),
            waits=[db_sp] + sp1[-2:] + sp2[-2:], dma=P.slot("cp") if t == 0 else P.slots[-1])
            for t in range(NT)]
        c_stores = cp
    elif NPHASES == 2:
        st1 = ExitStack()
        c_stores = emit_ffn(st1, dx, dout, "fa")
        st1.close()
    elif NPHASES == 3:
        st1 = ExitStack()
        emit_ffn(st1, dx, dX1, "fa")
        P.barrier()
        st1.close()
        st2x = ExitStack()
        c_stores = emit_attn(st2x, dX1, dout)
        st2x.close()
    else:
        st1 = ExitStack()
        emit_ffn(st1, dx, dX1, "fa")
        P.barrier()
        st1.close()

        st2x = ExitStack()
        emit_attn(st2x, dX1, dX2)
        P.barrier()
        st2x.close()

        st3 = ExitStack()
        c_stores = emit_ffn(st3, dX2, dout, "fc")
        st3.close()

    if DEBUG:
        sl_dbg = P.slot("dbg")
        dbgsx = nc.dram_tensor("dbg_sx", [2052], F32, kind="ExternalOutput")
        dbgs2 = nc.dram_tensor("dbg_s2", [T], F32, kind="ExternalOutput")
        dbgw1 = nc.dram_tensor("dbg_w1", [128, HFF], BF16, kind="ExternalOutput")
        dbgdb = nc.dram_tensor("dbg_db", [HEADS, 2560], BF16, kind="ExternalOutput")
        last = P.last
        ws = [o for o in last.values() if o is not None]
        P.op("gpsimd", lambda e: e.dma_start(out=dbgsx.ap()[0:2052],
                                             in_=dSX.ap()[0:2052]),
             waits=ws, dma=sl_dbg)
        P.op("gpsimd", lambda e: e.dma_start(out=dbgs2.ap()[0:T],
                                             in_=dS2.ap()[0:T]), dma=sl_dbg)
        P.op("gpsimd", lambda e: e.dma_start(out=dbgw1.ap()[:, :],
                                             in_=dWQ1.ap()[0, :, :]), dma=sl_dbg)
        P.op("gpsimd", lambda e: e.dma_start(out=dbgdb.ap()[:, :],
                                             in_=dDB.ap()[:, :]), dma=sl_dbg)
        P.op("gpsimd", None, waits=[P.last["gpsimd"]])

    P.op("sync", None, waits=c_stores)

    counts = P.emit(stack)
    return nc, stack, counts


_CACHE = {}


def kernel(**ins):
    xfull = np.ascontiguousarray(np.asarray(ins["L0_attn_x"], dtype=np.float32))
    w1t = np.ascontiguousarray(np.asarray(ins["L0_ffn_w1"], np.float32).T)
    w2t = np.ascontiguousarray(np.asarray(ins["L0_ffn_w2"], np.float32).T)
    cw = np.ascontiguousarray(
        np.asarray(ins["L0_ffn_conv_w"], np.float32)[:, 0, :]
        .reshape(NHC, 128, 3).transpose(2, 1, 0))
    cb = np.ascontiguousarray(
        np.asarray(ins["L0_ffn_conv_b"], np.float32).reshape(NHC, 128).T)
    common = {
        "wq": np.ascontiguousarray(np.asarray(ins["L0_attn_Wq"], np.float32)),
        "wk": np.ascontiguousarray(np.asarray(ins["L0_attn_Wk"], np.float32)),
        "wv": np.ascontiguousarray(np.asarray(ins["L0_attn_Wv"], np.float32)),
        "wo": np.ascontiguousarray(np.asarray(ins["L0_attn_Wo"], np.float32)),
        "bo": np.ascontiguousarray(np.asarray(ins["L0_attn_bo"], np.float32)),
        "rel": np.ascontiguousarray(np.asarray(ins["L0_attn_rel_emb"], np.float32)),
        "w1t": w1t, "w2t": w2t, "cw": cw, "cb": cb,
    }
    if "nc" not in _CACHE:
        nc, stck, counts = build_program()
        _CACHE["nc"] = nc
        _CACHE["stack"] = stck
        print("bass counts:", counts, "sbuf_free:", nc.sbuf_bytes_remaining)
    nc = _CACHE["nc"]
    in_maps = [dict(common, x=np.ascontiguousarray(xfull[i])) for i in range(8)]
    res = run_bass_kernel_spmd(nc, in_maps, core_ids=list(range(8)))
    return np.stack([res.results[i]["out"] for i in range(8)], axis=0)


# version 19
# speedup vs baseline: 1.0155x; 1.0155x over previous
import math
from contextlib import ExitStack

import numpy as np
import ml_dtypes

import concourse.bass as bass
import concourse.mybir as mybir
from concourse.bass_utils import run_bass_kernel_spmd

F32 = mybir.dt.float32
BF16 = mybir.dt.bfloat16
AF = mybir.ActivationFunctionType
ALU = mybir.AluOpType
AX = mybir.AxisListType.X
NBF = ml_dtypes.bfloat16

T = 2048
D = 1024
HFF = 4096
HEADS = 16
NT = 16
NDC = 8
NHC = 32
NCH = 4
MAGIC = 12582912.0
CEXP = 12.0

ENGS = ("tensor", "vector", "scalar", "gpsimd", "sync")


class _Slot:

    def __init__(self, name):
        self.name = name
        self.count = 0
        self.sem = None


class _Op:
    __slots__ = ("eng", "fn", "waits", "amt", "marked", "val", "slot", "small")

    def __init__(self, eng, fn, waits, amt, slot, small=False):
        self.eng, self.fn, self.waits, self.amt = eng, fn, waits, amt
        self.marked = False
        self.val = None
        self.slot = slot
        self.small = small


class Prog:

    def __init__(self, nc):
        self.nc = nc
        self.q = {e: [] for e in ENGS}
        self.last = {e: None for e in ENGS}
        self.slots = []
        self.slot_last = {}

    def slot(self, name):
        s = _Slot(name)
        self.slots.append(s)
        return s

    def op(self, eng, fn, waits=(), dma=None, small=False):
        ws = []
        for w in waits:
            if w is None:
                continue
            assert isinstance(w, _Op), f"bad wait {w!r}"
            if w.eng == eng and w.slot is None:
                continue
            if w not in ws:
                ws.append(w)
        o = _Op(eng, fn, ws, 16 if dma is not None else 1, dma, small=small)
        if dma is not None:
            o.marked = True
            dma.count += 16
            o.val = dma.count
            self.slot_last[id(dma)] = o
        self.q[eng].append(o)
        self.last[eng] = o
        return o

    def barrier(self):
        lasts = {e: self.last[e] for e in ENGS if self.last[e] is not None}
        slot_lasts = list(self.slot_last.values())
        for e in ENGS:
            ws = [o for e2, o in lasts.items() if e2 != e and o.slot is None]
            ws += slot_lasts
            if ws:
                self.op(e, None, waits=ws)

    CHAIN = ()

    def emit(self, stack):
        nc = self.nc
        for e in self.CHAIN:
            prev = None
            for o in self.q[e]:
                if o.slot is not None or o.fn is None:
                    continue
                if prev is not None and prev not in o.waits:
                    o.waits.append(prev)
                prev = o
        for e in ENGS:
            prev = None
            for o in self.q[e]:
                if o.slot is not None or o.fn is None:
                    continue
                if prev is not None and prev.small and prev not in o.waits:
                    o.waits.append(prev)
                prev = o
        for e in ENGS:
            for o in self.q[e]:
                for w in o.waits:
                    w.marked = True
        sems = {e: stack.enter_context(nc.semaphore(f"sem_{e}")) for e in ENGS}
        for s in self.slots:
            s.sem = stack.enter_context(nc.semaphore(f"slt_{s.name}"))
        for e in ENGS:
            c = 0
            for o in self.q[e]:
                if o.slot is None and o.marked:
                    c += o.amt
                    o.val = c
        spacer = stack.enter_context(nc.sbuf_tensor("spacer_junk", [128, 16], F32))
        block = stack.enter_context(nc.Block())

        def sem_of(o):
            return o.slot.sem if o.slot is not None else sems[o.eng]

        def key_of(o):
            return id(o.slot) if o.slot is not None else o.eng

        def mk(e):
            def body(eng):
                maxw = {}
                for o in self.q[e]:
                    need = {}
                    for w in o.waits:
                        k = key_of(w)
                        if maxw.get(k, -1) < w.val and need.get(k, -1) < w.val:
                            need[k] = w.val
                    for w in o.waits:
                        k = key_of(w)
                        if need.get(k, -1) == w.val:
                            eng.wait_ge(sem_of(w), w.val)
                            maxw[k] = w.val
                            del need[k]
                    if o.fn is None:
                        if o.marked and o.slot is None:
                            eng.nop().then_inc(sems[e], o.amt)
                        continue
                    inst = o.fn(eng)
                    if o.marked:
                        inst.then_inc(sem_of(o), o.amt)
            return body

        block.tensor(mk("tensor"))
        block.vector(mk("vector"))
        block.scalar(mk("scalar"))
        block.gpsimd(mk("gpsimd"))
        block.sync(mk("sync"))
        return {e: len(self.q[e]) for e in ENGS}


def _oh_const():
    n = np.arange(T)
    val = 16 + (np.log(np.maximum(n, 1).astype(np.float32) / np.float32(16.0))
                / np.float32(math.log(8.0)) * np.float32(16.0)).astype(np.int32)
    val = np.minimum(val, 31)
    buckets = np.where(n < 16, n, val)
    oh = np.zeros((33, 2560), np.float32)
    oh[buckets[::-1], np.arange(T)] = 1.0
    oh[31, :T] -= 1.0
    oh[32, T:] = 1.0
    return oh


NPHASES = 4
DEBUG = False


def build_program():
    nc = bass.Bass("TRN2", target_bir_lowering=False,
                   detect_race_conditions=False)
    stack = ExitStack()

    dx = nc.dram_tensor("x", [T, D], F32, kind="ExternalInput")
    dwq = nc.dram_tensor("wq", [D, D], F32, kind="ExternalInput")
    dwk = nc.dram_tensor("wk", [D, D], F32, kind="ExternalInput")
    dwv = nc.dram_tensor("wv", [D, D], F32, kind="ExternalInput")
    dwo = nc.dram_tensor("wo", [D, D], F32, kind="ExternalInput")
    dbo = nc.dram_tensor("bo", [D], F32, kind="ExternalInput")
    drel = nc.dram_tensor("rel", [32, HEADS], F32, kind="ExternalInput")
    dw1t = nc.dram_tensor("w1t", [D, HFF], F32, kind="ExternalInput")
    dw2t = nc.dram_tensor("w2t", [HFF, D], F32, kind="ExternalInput")
    dcw = nc.dram_tensor("cw", [3, 128, NHC], F32, kind="ExternalInput")
    dcb = nc.dram_tensor("cb", [128, NHC], F32, kind="ExternalInput")
    dout = nc.dram_tensor("out", [T, D], F32, kind="ExternalOutput")

    dX1 = nc.dram_tensor("X1", [T, D], F32)
    dX2 = nc.dram_tensor("X2", [T, D], F32)
    dWQ1 = nc.dram_tensor("WQ1", [NDC, 128, HFF], BF16)
    dWQ2 = nc.dram_tensor("WQ2", [NHC, 128, D], BF16)
    dDB = nc.dram_tensor("DB", [HEADS, 2560], BF16)
    dSX = nc.dram_tensor("SX", [2052], F32)
    dS2 = nc.dram_tensor("S2", [T], F32)
    dRD = nc.dram_tensor("RD", [32, 1024], F32)

    identb = nc.inline_tensor(np.eye(128, dtype=NBF), name="identb")
    identf = nc.inline_tensor(np.eye(128, dtype=np.float32), name="identf")
    onescol = nc.inline_tensor(np.ones((128, 1), np.float32), name="onescol")
    onesrow = nc.inline_tensor(np.ones((1, 128), np.float32), name="onesrow")
    doh = nc.inline_tensor(_oh_const(), name="ohc")

    P = Prog(nc)
    SB = lambda name, shape, dt_, st=stack: st.enter_context(
        nc.sbuf_tensor(name, shape, dt_))
    PS = lambda name, shape, dt_, st=stack: st.enter_context(
        nc.psum_tensor(name, shape, dt_))

    IDB = SB("idb", [128, 128], BF16)
    IDF = SB("idf", [128, 128], F32)
    ONC = SB("onc", [128, 1], F32)
    ONR = SB("onr", [1, 128], F32)
    TAP = [SB(f"tap{k}", [128, NHC], F32) for k in range(3)]
    CBT = SB("cbt", [128, NHC], F32)
    BC12 = SB("bc12", [128, 1], F32)
    S125 = SB("s125", [128, 1], F32)
    SW1I = SB("sw1i", [128, 1], F32)
    SW2I = SB("sw2i", [128, 1], F32)
    SW1V = SB("sw1v", [128, 1], F32)
    SW2V = SB("sw2v", [128, 1], F32)
    PB = [PS(f"pbank{i}", [128, 512], F32) for i in range(8)]

    sl_const = P.slot("consts")
    P.op("sync", lambda e: e.dma_start(out=IDB[:], in_=identb.ap()[:, :]), dma=sl_const)
    P.op("sync", lambda e: e.dma_start(out=IDF[:], in_=identf.ap()[:, :]), dma=sl_const)
    P.op("sync", lambda e: e.dma_start(out=ONC[:], in_=onescol.ap()[:, :]), dma=sl_const)
    P.op("sync", lambda e: e.dma_start(out=ONR[:], in_=onesrow.ap()[:, :]), dma=sl_const)
    P.op("sync", lambda e: e.dma_start(out=CBT[:], in_=dcb.ap()[:, :]), dma=sl_const)
    ld_all = None
    for k in range(3):
        ld_all = P.op("sync", lambda e, k=k: e.dma_start(
            out=TAP[k][:], in_=dcw.ap()[k, :, :]), dma=sl_const)
    ld_idb = ld_idf = ld_onc = ld_onr = ld_cbt = ld_all
    ld_tap = [ld_all, ld_all, ld_all]
    P.op("vector", lambda e: e.memset(BC12[:], -CEXP))
    P.op("vector", lambda e: e.memset(S125[:], 0.125))

    def emit_wquant(st, src, n_chunks, width, ddst, swv, swiv, extra, nm):
        wbuf = [SB(f"{nm}w{i}", [128, width], F32, st) for i in range(2)]
        qbuf = [SB(f"{nm}q{i}", [128, width], BF16, st) for i in range(2)]
        jabs = SB(f"{nm}ja", [128, width], F32, st)
        jrnd = SB(f"{nm}jr", [128, width], F32, st)
        acc = SB(f"{nm}ac", [128, n_chunks], F32, st)
        sums = SB(f"{nm}su", [128, 1], F32, st)
        sc = SB(f"{nm}sc", [1, 4], F32, st)
        sl_w = [P.slot(f"{nm}l{i}") for i in range(2)]
        sl_sp = [P.slot(f"{nm}sp{i}") for i in range(2)]
        absops = []
        for c in range(n_chunks):
            b = c % 2
            war = [absops[c - 2]] if c >= 2 else []
            ld = P.op("sync", lambda e, b=b, c=c: e.dma_start(
                out=wbuf[b][:], in_=src.ap()[c * 128:(c + 1) * 128, :]),
                waits=war, dma=sl_w[b])
            absops.append(P.op("scalar", lambda e, b=b, c=c: e.activation(
                jabs[:], wbuf[b][:], AF.Abs, accum_out=acc[:, c:c + 1]),
                waits=[ld]))
        rs = P.op("vector", lambda e, n_chunks=n_chunks: e.reduce_sum(
            sums[:], acc[:, 0:n_chunks], axis=AX), waits=absops)
        mt = P.op("tensor", lambda e: e.matmul(PB[0][0:1, 0:1], ONC[:], sums[:],
                                               start=True, stop=True),
                  waits=[rs, ld_onc])
        v1 = P.op("vector", lambda e, n_chunks=n_chunks, width=width:
                  e.tensor_scalar(sc[:, 0:1], PB[0][0:1, 0:1],
                                  1.0 / (n_chunks * 128 * width), 1e-5,
                                  ALU.mult, ALU.max), waits=[mt], small=True)
        v2 = P.op("vector", lambda e: e.reciprocal(sc[:, 1:2], sc[:, 0:1]),
                  small=True)
        v3 = P.op("vector", lambda e, extra=extra: e.tensor_scalar(
            sc[:, 2:3], sc[:, 0:1], extra, None, ALU.mult), small=True)
        mb = P.op("tensor", lambda e: e.matmul(
            PB[1][:, 0:2], ONR[:], sc[:, 1:3], start=True, stop=True),
            waits=[v2, v3, ld_onr])
        b1 = P.op("vector", lambda e, swv=swv: e.tensor_copy(
            swv[:], PB[1][:, 0:1]), waits=[mb], small=True)
        b2 = P.op("vector", lambda e, swiv=swiv: e.tensor_copy(
            swiv[:], PB[1][:, 1:2]), waits=[mb], small=True)
        spills = []
        rounds = []
        for c in range(n_chunks):
            b = c % 2
            war = [rounds[c - 2]] if c >= 2 else [absops[-1]]
            ld = P.op("sync", lambda e, b=b, c=c: e.dma_start(
                out=wbuf[b][:], in_=src.ap()[c * 128:(c + 1) * 128, :]),
                waits=war, dma=sl_w[b])
            r1 = P.op("vector", lambda e, b=b: e.tensor_scalar(
                jrnd[:], wbuf[b][:], swv[:], None, ALU.mult), waits=[b1, ld])
            rounds.append(r1)
            P.op("vector", lambda e: e.tensor_scalar(
                jrnd[:], jrnd[:], MAGIC, MAGIC, ALU.add, ALU.subtract))
            r3 = P.op("vector", lambda e, b=b: e.tensor_scalar(
                qbuf[b][:], jrnd[:], 1.0, -1.0, ALU.min, ALU.max),
                waits=[spills[c - 2]] if c >= 2 else [])
            spills.append(P.op("gpsimd", lambda e, b=b, c=c: e.dma_start(
                out=ddst.ap()[c, :, :], in_=qbuf[b][:]), waits=[r3],
                dma=sl_sp[b]))
        return spills, b2

    wst = ExitStack()
    sp1, _ = emit_wquant(wst, dw1t, NDC, HFF, dWQ1, SW1V, SW1I, 1.0 / 127.0, "w1")
    sp2, _ = emit_wquant(wst, dw2t, NHC, D, dWQ2, SW2V, SW2I, 0.5 / 127.0, "w2")

    relx = SB("relx", [33, HEADS], F32, wst)
    ohs = SB("ohs", [33, 2560], F32, wst)
    dbs = SB("dbs", [HEADS, 2560], BF16, wst)
    sl_dbin = P.slot("dbin")
    sl_dbsp = P.slot("dbsp")
    r1 = P.op("sync", lambda e: e.dma_start(out=relx[0:32, :], in_=drel.ap()[:, :]), dma=sl_dbin)
    r2 = P.op("vector", lambda e: e.memset(relx[32:33, :], -30000.0))
    r3 = P.op("sync", lambda e: e.dma_start(out=ohs[:], in_=doh.ap()[:, :]), dma=sl_dbin)
    dbev = []
    for i in range(5):
        mm = P.op("tensor", lambda e, i=i: e.matmul(
            PB[2 + i][0:HEADS, :], relx[:], ohs[:, i * 512:(i + 1) * 512],
            start=True, stop=True), waits=[r1, r2, r3])
        dbev.append(P.op("scalar", lambda e, i=i: e.activation(
            dbs[:, i * 512:(i + 1) * 512], PB[2 + i][0:HEADS, :], AF.Exp),
            waits=[mm]))
    db_sp = P.op("gpsimd", lambda e: e.dma_start(out=dDB.ap()[:, :], in_=dbs[:]),
                 waits=dbev, dma=sl_dbsp)
    P.barrier()
    wst.close()

    def emit_ln(SBL, src_dram, quant, XQdst, Ydst, srow, zrow, sx_sp_dram, jnk,
                tag):
        xt = [SBL(f"xt{i}", [128, D], F32) for i in range(4)]
        yt = [SBL(f"yt{i}", [128, D], F32) for i in range(2)] if quant else None
        qt = SBL("qt", [128, D], F32) if quant else None
        qb = [SBL(f"qb{i}", [128, D], BF16) for i in range(2)]
        stats = SBL("sta", [128, 16], F32)
        statq = SBL("stq", [128, 16], F32)
        sl_xt = [P.slot(f"{tag}xt{i}") for i in range(4)]
        sl_sx = P.slot(f"{tag}sxs") if quant else None
        prev_cons = {}
        prev_q1 = {}
        prev_tr = {}
        prev_d3 = None
        tr_list = []
        last_sx = None
        tr_flip = [0]
        zs = None
        zm = []
        if quant:
            z4 = P.op("vector", lambda e: e.memset(zrow[:], 0.0))
            z1 = P.op("gpsimd", lambda e: e.dma_start(
                out=sx_sp_dram.ap()[0:1], in_=zrow[0:1, 0:1]),
                waits=[z4], dma=sl_sx)
            zs = P.op("gpsimd", lambda e: e.dma_start(
                out=sx_sp_dram.ap()[2049:2052], in_=zrow[0:1, 0:3]),
                waits=[z4], dma=sl_sx)
            zm = [P.op("gpsimd", lambda e, d=d: e.memset(XQdst[d][:, 0:1], 0))
                  for d in range(NDC)]
            zm += [P.op("gpsimd", lambda e, d=d: e.memset(
                XQdst[d][:, 2049:2052], 0)) for d in range(NDC)]
        for tc in range(NCH):
            red, sq = [], []
            for j in range(4):
                tt = tc * 4 + j
                ld = P.op("sync", lambda e, j=j, tt=tt: e.dma_start(
                    out=xt[j][:], in_=src_dram.ap()[tt * 128:(tt + 1) * 128, :]),
                    waits=[prev_cons.get(j)], dma=sl_xt[j])
                red.append(P.op("vector", lambda e, j=j: e.reduce_sum(
                    stats[:, j:j + 1], xt[j][:], axis=AX),
                    waits=[ld, prev_d3], small=True))
                sq.append(P.op("scalar", lambda e, j=j: e.activation(
                    jnk[:], xt[j][:], AF.Square,
                    accum_out=stats[:, 4 + j:5 + j]), waits=[ld, prev_d3]))
            P.op("vector", lambda e: e.tensor_scalar(
                stats[:, 8:12], stats[:, 0:4], -1.0 / D, None, ALU.mult),
                waits=red, small=True)
            P.op("vector", lambda e: e.tensor_scalar(
                stats[:, 12:16], stats[:, 4:8], 1.0 / D, None, ALU.mult),
                waits=sq, small=True)
            P.op("vector", lambda e: e.tensor_tensor(
                statq[:, 0:4], stats[:, 8:12], stats[:, 8:12], ALU.mult),
                small=True)
            P.op("vector", lambda e: e.tensor_tensor(
                statq[:, 4:8], stats[:, 12:16], statq[:, 0:4], ALU.subtract),
                small=True)
            P.op("vector", lambda e: e.tensor_scalar(
                statq[:, 8:12], statq[:, 4:8], 1e-5, None, ALU.add),
                small=True)
            c5 = P.op("vector", lambda e: e.reciprocal(
                statq[:, 12:16], statq[:, 8:12]), small=True)
            c6 = P.op("scalar", lambda e: e.sqrt(stats[:, 0:4],
                                                 statq[:, 12:16]), waits=[c5])
            P.op("vector", lambda e: e.tensor_tensor(
                stats[:, 4:8], stats[:, 0:4], stats[:, 0:4], ALU.mult),
                waits=[c6], small=True)
            P.op("vector", lambda e: e.tensor_tensor(
                statq[:, 0:4], stats[:, 4:8], statq[:, 8:12], ALU.mult),
                small=True)
            P.op("vector", lambda e: e.tensor_scalar(
                statq[:, 4:8], statq[:, 0:4], -0.5, 1.5, ALU.mult, ALU.add),
                small=True)
            c10 = P.op("vector", lambda e: e.tensor_tensor(
                stats[:, 12:16], stats[:, 0:4], statq[:, 4:8], ALU.mult),
                small=True)
            c11 = P.op("vector", lambda e: e.tensor_tensor(
                statq[:, 8:12], stats[:, 8:12], stats[:, 12:16], ALU.mult),
                small=True)
            u2s = []
            for j in range(4):
                tt = tc * 4 + j
                b = tt % 2
                if quant:
                    ya = P.op("scalar", lambda e, b=b, j=j: e.activation(
                        yt[b][:], xt[j][:], AF.Identity, bias=statq[:, 8 + j:9 + j],
                        scale=stats[:, 12 + j:13 + j]),
                        waits=[c10, c11, prev_q1.get(b)])
                    mxo = P.op("vector", lambda e, b=b, j=j: e.reduce_max(
                        statq[:, 12 + j:13 + j], yt[b][:], axis=AX,
                        apply_absolute_value=True), waits=[ya], small=True)
                    u0 = P.op("vector", lambda e, j=j: e.tensor_scalar(
                        stats[:, 0 + j:1 + j], statq[:, 12 + j:13 + j],
                        1e-5, None, ALU.max), waits=[mxo], small=True)
                    u1 = P.op("vector", lambda e, j=j: e.reciprocal(
                        stats[:, 4 + j:5 + j], stats[:, 0 + j:1 + j]),
                        small=True)
                    u2 = P.op("vector", lambda e, j=j: e.tensor_scalar(
                        statq[:, 0 + j:1 + j], stats[:, 4 + j:5 + j],
                        127.0, None, ALU.mult), small=True)
                    u2s.append(u2)
                    q1 = P.op("vector", lambda e, b=b, j=j: e.tensor_scalar(
                        qt[:], yt[b][:], statq[:, 0 + j:1 + j], MAGIC,
                        ALU.mult, ALU.add), waits=[u2])
                    prev_q1[b] = q1
                    q2 = P.op("vector", lambda e, b=b: e.tensor_scalar(
                        qb[b][:], qt[:], MAGIC, None, ALU.subtract),
                        waits=[prev_tr.get(b)])
                    prev_cons[j] = q2
                    src_t = qb[b]
                else:
                    ya = P.op("scalar", lambda e, b=b, j=j: e.activation(
                        qb[b][:], xt[j][:], AF.Identity, bias=statq[:, 8 + j:9 + j],
                        scale=stats[:, 12 + j:13 + j]),
                        waits=[c10, c11, prev_tr.get(b)])
                    prev_cons[j] = ya
                    q2 = ya
                    src_t = qb[b]
                trs = None
                for dc in range(NDC):
                    f = tr_flip[0] % 2
                    pb_ap = PB[5 + f].ap()[:, 0:64].bitcast(BF16)
                    tr1 = P.op("tensor",
                               lambda e, dc=dc, pb_ap=pb_ap, src_t=src_t:
                               e.transpose(pb_ap,
                                           src_t[:, dc * 128:(dc + 1) * 128],
                                           IDB[:]),
                               waits=[q2, ld_idb]
                               + ([tr_list[-2]] if len(tr_list) >= 2 else []))
                    if quant:
                        tr2 = P.op("vector",
                                   lambda e, dc=dc, tt=tt, pb_ap=pb_ap:
                                   e.tensor_copy(
                                       XQdst[dc][:, 1 + tt * 128:1 + (tt + 1) * 128],
                                       pb_ap), waits=[tr1] + zm)
                    else:
                        tr2 = P.op("vector",
                                   lambda e, dc=dc, tt=tt, pb_ap=pb_ap:
                                   e.tensor_copy(
                                       Ydst[dc][:, tt * 128:(tt + 1) * 128],
                                       pb_ap), waits=[tr1])
                    tr_list.append(tr2)
                    trs = tr2
                    tr_flip[0] += 1
                prev_tr[b] = trs
            if quant:
                d3 = P.op("vector", lambda e: e.tensor_scalar(
                    stats[:, 4:8], stats[:, 0:4], SW1I[:], None, ALU.mult),
                    waits=u2s, small=True)
                trx = P.op("tensor", lambda e: e.transpose(
                    PB[7][0:4, 0:128], stats[:, 4:8], IDF[:]),
                    waits=[d3, ld_idf, last_sx])
                prev_d3 = trx
                cp4 = P.op("vector", lambda e: e.tensor_copy(
                    srow[:], PB[7][0:4, 0:128]), waits=[trx, last_sx])
                last_sx = P.op("gpsimd", lambda e, tc=tc: e.dma_start(
                    out=sx_sp_dram.ap()[1 + tc * 512:1 + (tc + 1) * 512],
                    in_=srow[:]), waits=[cp4, zs], dma=sl_sx)
        sx_sp = last_sx
        return tr_list[-1], sx_sp

    def emit_ffn(st, src_dram, dst_dram, tag):
        SBL = lambda n, sh, dt_: SB(f"{tag}{n}", sh, dt_, st)
        XQ = [SBL(f"xq{d}", [128, 2052], BF16) for d in range(NDC)]
        srow = SBL("srow", [4, 128], F32)
        zrow = SBL("zrow", [1, 4], F32)
        GB = [SBL(f"gb{h}", [128, 515], F32) for h in range(NHC)]
        CT = [SBL(f"ct{h}", [128, 3], F32) for h in range(NHC)]
        W1S = [[SBL(f"w1s{b}_{d}", [128, 512], BF16) for d in range(NDC)]
               for b in range(2)]
        W2S = [[SBL(f"w2s{b}_{j}", [128, D], BF16) for j in range(4)]
               for b in range(3)]
        sxb = [SBL(f"sxb{i}", [128, 512], F32) for i in range(2)]
        sxp = SBL("sxp", [128, 2], F32)
        s2b = SBL("s2b", [128, 512], F32)
        cva = SBL("cva", [128, 512], F32)
        cvb = SBL("cvb", [128, 512], F32)
        tqt = SBL("tq", [128, 512], F32)
        MX = SBL("mx", [128, 512], F32)
        MN = SBL("mn", [128, 512], F32)
        st2 = SBL("st2", [128, 16], F32)
        s2s4 = SBL("s2s4", [32, 128], F32)
        xr = [SBL(f"xr{i}", [128, D], F32) for i in range(2)]
        xo = [SBL(f"xo{i}", [128, D], F32) for i in range(2)]
        hqv = [GB[h].ap()[:, 0:256].bitcast(BF16) for h in range(NHC)]
        for h in range(NHC):
            assert tuple(hqv[h].shape) == (128, 512), hqv[h].shape

        ct0 = [P.op("vector", lambda e, h=h: e.memset(CT[h][:, 0:1], 0.0))
               for h in range(NHC)]
        s2z = P.op("vector", lambda e: e.memset(s2s4[:], 0.0))
        sl_sxb = P.slot(f"{tag}sxb")
        sl_sxp = P.slot(f"{tag}sxp")
        sl_w1 = [P.slot(f"{tag}w1{i}") for i in range(3)]
        sl_w2 = [P.slot(f"{tag}w2{i}") for i in range(3)]
        sl_s2s = P.slot(f"{tag}s2s")
        sl_s2b = P.slot(f"{tag}s2b")
        sl_xr = [P.slot(f"{tag}xr{i}") for i in range(2)]
        sl_xo = [P.slot(f"{tag}xo{i}") for i in range(2)]
        xq_done, sx_sp = emit_ln(SBL, src_dram, True, XQ, None, srow, zrow,
                                 dSX, xo[0], tag)

        geighth = [0]
        g2idx = [0]
        mm2_of_gen = {}
        last_mm1_of_eighth = {}
        prev_sxl = None
        prev_s2l = None
        prev_mm2ev = {k: None for k in range(8)}
        hq_ops = {}
        preev = []
        store_ops = []
        prev_trh = []
        prev_s2copy = None
        psxl = {}
        for c in range(NCH):
            sxl = P.op("gpsimd", lambda e, c=c: e.dma_start(
                out=sxb[c % 2][:],
                in_=bass.AP(dSX, 512 * c + 3, [[0, 128], [1, 512]])),
                waits=[sx_sp, psxl.get(c % 2)], dma=sl_sxb)
            if c == 0:
                sxpl = P.op("gpsimd", lambda e: e.dma_start(
                    out=sxp[:], in_=bass.AP(dSX, 1, [[0, 128], [1, 2]])),
                    waits=[sx_sp], dma=sl_sxp)
            mm1ev = {}
            mx_folds = []
            mn_folds = []
            mzero = P.op("vector", lambda e: e.memset(MX[:], 0.0),
                         waits=prev_trh)
            nzero = P.op("vector", lambda e: e.memset(MN[:], 0.0),
                         waits=prev_trh)
            conv_sil = []
            cur_lds = None
            cur_gb = 0
            for hc in range(NHC):
                eh = hc // 4
                if hc % 4 == 0:
                    g = geighth[0]
                    cur_gb = g % 2
                    war = [last_mm1_of_eighth.get(g - 2)]
                    cur_lds = [P.op("sync" if dn < 4 else "scalar",
                                    lambda e, gb=cur_gb, dn=dn, eh=eh:
                                    e.dma_start(
                                        out=W1S[gb][dn][:],
                                        in_=dWQ1.ap()[dn, :, eh * 512:(eh + 1) * 512]),
                                    waits=war + sp1[-2:],
                                    dma=sl_w1[cur_gb])
                               for dn in range(NDC)]
                    geighth[0] += 1
                bank = hc % 2
                if c == 0:
                    pmm = None
                    pwar = [preev[hc - 2]] if hc >= 2 else []
                    for dn in range(NDC):
                        pmm = P.op("tensor",
                                   lambda e, dn=dn, hc=hc, gb=cur_gb:
                                   e.matmul(
                                       PB[2 + hc % 2][:, 0:2],
                                       W1S[gb][dn][:, (hc % 4) * 128:(hc % 4 + 1) * 128],
                                       XQ[dn][:, 1:3],
                                       start=(dn == 0), stop=(dn == NDC - 1)),
                                   waits=cur_lds + [xq_done] + pwar)
                    preev.append(P.op("vector", lambda e, hc=hc: e.tensor_tensor(
                        CT[hc][:, 1:3],
                        PB[2 + hc % 2][:, 0:2],
                        sxp[:], ALU.mult), waits=[pmm, sxpl], small=True))
                mm = None
                for dn in range(NDC):
                    mm = P.op("tensor",
                              lambda e, dn=dn, hc=hc, bank=bank, c=c, gb=cur_gb:
                              e.matmul(
                                  PB[bank][:],
                                  W1S[gb][dn][:, (hc % 4) * 128:(hc % 4 + 1) * 128],
                                  XQ[dn][:, 512 * c + 3:512 * c + 515],
                                  start=(dn == 0), stop=(dn == NDC - 1)),
                              waits=cur_lds + [xq_done, mm1ev.get(hc - 2),
                                               prev_mm2ev[bank]])
                last_mm1_of_eighth[geighth[0] - 1] = mm
                war_gb = hq_ops.get((c - 1, hc))
                ev = P.op("vector", lambda e, hc=hc, bank=bank, c=c:
                          e.tensor_tensor(
                    GB[hc][:, 3:515], PB[bank][:], sxb[c % 2][:], ALU.mult),
                    waits=[mm, sxl, war_gb])
                mm1ev[hc] = ev
                cin = P.op("vector", lambda e, hc=hc: e.tensor_copy(
                    GB[hc][:, 0:3], CT[hc][:]),
                    waits=[ct0[hc], war_gb] + ([preev[hc]] if c == 0 else []),
                    small=True)
                cout = P.op("vector", lambda e, hc=hc: e.tensor_copy(
                    CT[hc][:], GB[hc][:, 512:515]), waits=[ev], small=True)
                cc = P.op("scalar", lambda e, hc=hc: e.activation(
                    cva[:], GB[hc][:, 1:513], AF.Identity,
                    bias=CBT[:, hc:hc + 1], scale=TAP[1][:, hc:hc + 1]),
                    waits=[ev, cin, ld_cbt, ld_tap[1]])
                s1 = P.op("vector", lambda e, hc=hc:
                          e.scalar_tensor_tensor(
                              cvb[:], GB[hc][:, 0:512], TAP[0][:, hc:hc + 1],
                              cva[:], ALU.mult, ALU.add),
                          waits=[cc, ld_tap[0]])
                s2_ = P.op("vector", lambda e, hc=hc:
                           e.scalar_tensor_tensor(
                               cva[:], GB[hc][:, 2:514], TAP[2][:, hc:hc + 1],
                               cvb[:], ALU.mult, ALU.add),
                           waits=[ld_tap[2]])
                sil = P.op("scalar", lambda e, hc=hc: e.activation(
                    GB[hc][:, 0:512], cva[:], AF.Silu), waits=[s2_, cout])
                conv_sil.append(sil)
                mx_folds.append(P.op("vector", lambda e, hc=hc: e.tensor_tensor(
                    MX[:], MX[:], GB[hc][:, 0:512], ALU.max),
                    waits=[sil, mzero]))
                mn_folds.append(P.op("vector", lambda e, hc=hc: e.tensor_tensor(
                    MN[:], MN[:], GB[hc][:, 0:512], ALU.min),
                    waits=[sil, nzero]))
            trh_reads = []
            for i in range(4):
                trh = P.op("tensor", lambda e, i=i: e.transpose(
                    PB[3][:, 0:128], MX[:, i * 128:(i + 1) * 128], IDF[:]),
                    waits=mx_folds + [ld_idf, prev_mm2ev[3]]
                    + ([trh_reads[-1]] if trh_reads else []))
                rd_ = P.op("vector", lambda e, i=i: e.reduce_max(
                    st2[:, 0 + i:1 + i], PB[3][:, 0:128], axis=AX), waits=[trh],
                    small=True)
                trn = P.op("tensor", lambda e, i=i: e.transpose(
                    PB[3][:, 0:128], MN[:, i * 128:(i + 1) * 128], IDF[:]),
                    waits=mn_folds + [rd_])
                rdn = P.op("vector", lambda e, i=i: e.tensor_reduce(
                    st2[:, 12 + i:13 + i], PB[3][:, 0:128], AX, ALU.min),
                    waits=[trn], small=True)
                trh_reads.append(rdn)
            negn = P.op("vector", lambda e: e.tensor_scalar(
                st2[:, 4:8], st2[:, 12:16], -1.0, None, ALU.mult),
                waits=trh_reads, small=True)
            pmax = P.op("vector", lambda e: e.tensor_tensor(
                st2[:, 8:12], st2[:, 0:4], st2[:, 4:8], ALU.max), small=True)
            prev_trh = trh_reads + [pmax]
            e0 = P.op("vector", lambda e: e.tensor_scalar(
                st2[:, 4:8], st2[:, 8:12], 1e-5, None, ALU.max), waits=[pmax],
                small=True)
            P.op("vector", lambda e: e.reciprocal(st2[:, 0:4], st2[:, 4:8]),
                 small=True)
            e2 = P.op("vector", lambda e: e.tensor_scalar(
                st2[:, 12:16], st2[:, 0:4], 127.0, None, ALU.mult), small=True)
            e3 = P.op("vector", lambda e: e.tensor_scalar(
                st2[:, 8:12], st2[:, 4:8], SW2I[:], None, ALU.mult), small=True)
            trs2 = P.op("tensor", lambda e: e.transpose(
                PB[7][0:4, 0:128], st2[:, 12:16], IDF[:]),
                waits=[e2, prev_s2copy, ld_idf, prev_mm2ev[7]])
            s2st = P.op("vector", lambda e: e.tensor_copy(
                s2s4[0:4, :], PB[7][0:4, 0:128]), waits=[trs2, s2z, prev_s2copy])
            s2shf = []
            for j in range(4):
                for dq in range(4):
                    s2shf.append(P.op("vector",
                        lambda e, j=j, dq=dq: e.stream_shuffle(
                            s2b[dq * 32:(dq + 1) * 32, j * 128:(j + 1) * 128],
                            s2s4[0:32, :], [j] * 32),
                        waits=[s2st, prev_s2l]))
            prev_s2copy = s2shf[-1]
            s2l = P.op("vector", lambda e: e.nop(), waits=s2shf)
            hq_list = []
            for hc in range(NHC):
                t1 = P.op("vector", lambda e, hc=hc: e.tensor_tensor(
                    tqt[:], GB[hc][:, 0:512], s2b[:], ALU.mult),
                    waits=[s2l, mx_folds[hc]])
                t2 = P.op("vector", lambda e, hc=hc: e.tensor_scalar(
                    hqv[hc], tqt[:], MAGIC, MAGIC, ALU.add, ALU.subtract))
                hq_ops[(c, hc)] = t2
                hq_list.append(t2)
            mm2 = {k: None for k in range(8)}
            for g in range(8):
                gg = g2idx[0]
                gb2 = gg % 3
                war = mm2_of_gen.get(gg - 3, [])
                l2 = [P.op("sync" if j < 2 else "scalar",
                           lambda e, gb2=gb2, j=j, g=g: e.dma_start(
                    out=W2S[gb2][j][:], in_=dWQ2.ap()[g * 4 + j, :, :]),
                    waits=war + sp2[-2:], dma=sl_w2[gb2]) for j in range(4)]
                mm2_of_gen[gg] = []
                g2idx[0] += 1
                for j in range(4):
                    hc = g * 4 + j
                    for tl in range(4):
                        for dh in range(2):
                            k = tl * 2 + dh
                            extra = []
                            if hc == 0:
                                extra.append(prev_mm2ev[k])
                                if k < 2:
                                    extra.append(mm1ev.get(NHC - 2 + k))
                            mm2[k] = P.op("tensor",
                                          lambda e, hc=hc, tl=tl, dh=dh, k=k,
                                          gb2=gb2, j=j:
                                          e.matmul(
                                              PB[k][:],
                                              hqv[hc][:, tl * 128:(tl + 1) * 128],
                                              W2S[gb2][j][:, dh * 512:(dh + 1) * 512],
                                              start=(hc == 0),
                                              stop=(hc == NHC - 1)),
                                          waits=[hq_list[hc]] + l2 + extra)
                            mm2_of_gen[gg].append(mm2[k])
            for tl in range(4):
                b = tl % 2
                tt = c * 4 + tl
                lxr = P.op("sync", lambda e, b=b, tt=tt: e.dma_start(
                    out=xr[b][:], in_=src_dram.ap()[tt * 128:(tt + 1) * 128, :]),
                    waits=[store_ops[-2]] if len(store_ops) >= 2 else [],
                    dma=sl_xr[b])
                evs = []
                for dh in range(2):
                    k = tl * 2 + dh
                    ev2 = P.op("vector", lambda e, k=k, b=b, dh=dh, tl=tl:
                               e.scalar_tensor_tensor(
                                   xo[b][:, dh * 512:(dh + 1) * 512], PB[k][:],
                                   st2[:, 8 + tl:9 + tl],
                                   xr[b][:, dh * 512:(dh + 1) * 512],
                                   ALU.mult, ALU.add),
                               waits=[mm2[k], lxr, e3,
                                      store_ops[-2] if len(store_ops) >= 2 else None])
                    prev_mm2ev[k] = ev2
                    evs.append(ev2)
                store_ops.append(P.op("gpsimd", lambda e, b=b, tt=tt:
                                      e.dma_start(
                                          out=dst_dram.ap()[tt * 128:(tt + 1) * 128, :],
                                          in_=xo[b][:]), waits=evs, dma=sl_xo[b]))
            psxl[c % 2] = mm1ev[NHC - 1]
            prev_s2l = hq_list[-1]
        return store_ops

    def emit_attn(st, src_dram, dst_dram, tag="a"):
        SBL = lambda n, sh, dt_: SB(f"{tag}{n}", sh, dt_, st)
        Y = [SBL(f"y{d}", [128, T], BF16) for d in range(NDC)]
        Q = [SBL(f"q{d}", [128, T], BF16) for d in range(NDC)]
        K = [SBL(f"k{d}", [128, T], BF16) for d in range(NDC)]
        VE = [SBL(f"ve{t}", [128, 1040], BF16) for t in range(NT)]
        WB = [SBL(f"wb{d}", [128, D], BF16) for d in range(NDC)]
        RA = [SBL(f"ra{i}", [128, 1024], BF16) for i in range(2)]
        RB = [SBL(f"rb{i}", [128, 1024], BF16) for i in range(2)]
        PAb = [SBL(f"pa{i}", [128, 512], BF16) for i in range(2)]
        PBb = [SBL(f"pt{i}", [128, 512], BF16) for i in range(2)]
        rdr = SBL("rdr", [128, 1024], F32)
        bob = rdr
        xr = [SBL(f"xr{i}", [128, D], F32) for i in range(2)]
        xo = [SBL(f"xo{i}", [128, D], F32) for i in range(2)]
        t1b = SBL("t1b", [128, 512], F32)
        AC = Y

        sl_bo = P.slot(f"{tag}bo")
        sl_wl = P.slot(f"{tag}wl")
        sl_ra = [P.slot(f"{tag}ra{i}") for i in range(2)]
        sl_rb = [P.slot(f"{tag}rbs{i}") for i in range(2)]
        sl_rd1 = P.slot(f"{tag}rd1")
        sl_rd2 = P.slot(f"{tag}rd2")
        sl_rd3 = P.slot(f"{tag}rd3")
        sl_xr = [P.slot(f"{tag}xr{i}") for i in range(2)]
        sl_xo = [P.slot(f"{tag}xo{i}") for i in range(2)]
        ld_bo = None
        rz = [P.op("vector", lambda e, tt=tt: e.memset(tt[64:96, :], 0.0))
              for tt in (rdr, xr[0])]
        vz = []
        for t in range(NT):
            for h2 in range(16):
                vz.append(P.op("gpsimd", lambda e, t=t, h2=h2: e.memset(
                    VE[t][:, h2 * 65 + 64:h2 * 65 + 65], 1.0)))
        vz_last = vz[-1]

        y_done, _ = emit_ln(SBL, src_dram, False, None, Y, None, None, None,
                            xo[0], tag)

        def load_w(dsrc, waits):
            return [P.op("gpsimd", lambda e, dc=dc, dsrc=dsrc: e.dma_start(
                out=WB[dc][:], in_=dsrc.ap()[dc * 128:(dc + 1) * 128, :]),
                waits=waits, dma=sl_wl) for dc in range(NDC)]

        qk_ev = []
        lw = load_w(dwq, [])
        for dc in range(NDC):
            for qc in range(NCH):
                bank = (dc * NCH + qc) % 2
                mm = None
                for dn in range(NDC):
                    mm = P.op("tensor", lambda e, dn=dn, dc=dc, qc=qc, bank=bank:
                              e.matmul(PB[bank][:],
                                       WB[dn][:, dc * 128:(dc + 1) * 128],
                                       Y[dn][:, qc * 512:(qc + 1) * 512],
                                       start=(dn == 0), stop=(dn == NDC - 1)),
                              waits=lw + [y_done]
                              + ([qk_ev[-2]] if len(qk_ev) >= 2 else []))
                qk_ev.append(P.op("scalar", lambda e, dc=dc, qc=qc, bank=bank:
                                  e.activation(Q[dc][:, qc * 512:(qc + 1) * 512],
                                               PB[bank][:], AF.Identity,
                                               scale=S125[:]), waits=[mm]))
        lw2 = load_w(dwk, [qk_ev[-1]])
        kv_ev = []
        for dc in range(NDC):
            for qc in range(NCH):
                bank = (dc * NCH + qc) % 2
                mm = None
                for dn in range(NDC):
                    mm = P.op("tensor", lambda e, dn=dn, dc=dc, qc=qc, bank=bank:
                              e.matmul(PB[bank][:],
                                       WB[dn][:, dc * 128:(dc + 1) * 128],
                                       Y[dn][:, qc * 512:(qc + 1) * 512],
                                       start=(dn == 0), stop=(dn == NDC - 1)),
                              waits=lw2
                              + ([kv_ev[-2]] if len(kv_ev) >= 2 else []))
                kv_ev.append(P.op("scalar", lambda e, dc=dc, qc=qc, bank=bank:
                                  e.activation(K[dc][:, qc * 512:(qc + 1) * 512],
                                               PB[bank][:], AF.Copy),
                                  waits=[mm]))
        lw3 = load_w(dwv, [kv_ev[-1]])
        v_ev = []
        for t in range(NT):
            for dhf in range(2):
                bank = (t * 2 + dhf) % 2
                mm = None
                for dn in range(NDC):
                    mm = P.op("tensor", lambda e, dn=dn, t=t, dhf=dhf, bank=bank:
                              e.matmul(PB[bank][:],
                                       Y[dn][:, t * 128:(t + 1) * 128],
                                       WB[dn][:, dhf * 512:(dhf + 1) * 512],
                                       start=(dn == 0), stop=(dn == NDC - 1)),
                              waits=lw3
                              + ([v_ev[-9]] if len(v_ev) >= 16 else []))
                for hl in range(8):
                    h = dhf * 8 + hl
                    slot = (h // 2) * 130 + (h % 2) * 65
                    v_ev.append(P.op("scalar",
                                     lambda e, t=t, hl=hl, slot=slot, bank=bank:
                                     e.activation(VE[t][:, slot:slot + 64],
                                                  PB[bank][:, hl * 64:(hl + 1) * 64],
                                                  AF.Copy),
                                     waits=[mm, vz_last]))
        v_done = v_ev[-1]

        norm_ops = []
        bank_rd = {}
        pab_rd = {}
        ra_rd = {0: [], 1: []}
        pv_rd = {}
        d1_hist = {}
        rb_rd = {}
        last_rload = None
        prev_d1 = None
        for pr in range(8):
            rp = pr % 2
            rlA = P.op("gpsimd", lambda e, pr=pr, rp=rp: e.dma_start(
                out=RA[rp][:], in_=bass.AP(dDB, (2 * pr) * 2560 + 1408,
                                           [[1, 128], [1, 1024]])),
                waits=[db_sp, last_rload] + ra_rd[rp], dma=sl_ra[rp])
            rlB = P.op("gpsimd", lambda e, pr=pr, rp=rp: e.dma_start(
                out=RB[rp][:], in_=bass.AP(dDB, (2 * pr + 1) * 2560 + 1408,
                                           [[1, 128], [1, 1024]])),
                waits=[db_sp, last_rload] + ra_rd[rp], dma=sl_rb[rp])
            last_rload = rlB
            ra_rd[rp] = []
            for qc in range(NCH):
                nkt = 4 * qc + 4
                p = qc % 2
                pvA = (2, 0)[p]
                pvB = pvA + 1
                oa_mm = None
                ob_mm = None
                for kt in range(nkt):
                    kb = kt % 2
                    near = kt >= 4 * qc - 1
                    j0 = 2047 - 512 * qc + 128 * kt
                    bA = 4 + 2 * kb
                    bB = bA + 1
                    we = bank_rd.get(kb, (None, None))
                    mA = P.op("tensor", lambda e, pr=pr, qc=qc, kt=kt, bA=bA:
                              e.matmul(PB[bA][:],
                                       K[pr][0:64, kt * 128:(kt + 1) * 128],
                                       Q[pr][0:64, qc * 512:(qc + 1) * 512],
                                       start=True, stop=True,
                                       tile_position=(0, 0)),
                              waits=[qk_ev[-1], kv_ev[-1], we[0]])
                    mB = P.op("tensor", lambda e, pr=pr, qc=qc, kt=kt, bB=bB:
                              e.matmul(PB[bB][:],
                                       K[pr][64:128, kt * 128:(kt + 1) * 128],
                                       Q[pr][64:128, qc * 512:(qc + 1) * 512],
                                       start=True, stop=True,
                                       tile_position=(64, 0)),
                              waits=[we[1]])
                    wpab = pab_rd.get(kb, (None, None))
                    eA = P.op("scalar", lambda e, kb=kb, bA=bA: e.activation(
                        PAb[kb][:], PB[bA][:], AF.Exp, bias=BC12[:]),
                        waits=[mA, wpab[0]])
                    eB = P.op("scalar", lambda e, kb=kb, bB=bB: e.activation(
                        PBb[kb][:], PB[bB][:], AF.Exp, bias=BC12[:]),
                        waits=[mB, wpab[1]])
                    bank_rd[kb] = (eA, eB)
                    if near:
                        pA = P.op("vector", lambda e, kb=kb, j0=j0, rp=rp:
                                  e.tensor_mul(
                                      PAb[kb][:], PAb[kb][:],
                                      bass.AP(RA[rp], j0 - 1408, [[1024, 128], [-1, 512]])),
                                  waits=[eA, rlA])
                        pB = P.op("vector", lambda e, kb=kb, j0=j0, rp=rp:
                                  e.tensor_mul(
                                      PBb[kb][:], PBb[kb][:],
                                      bass.AP(RB[rp], j0 - 1408, [[1024, 128], [-1, 512]])),
                                  waits=[eB, rlB])
                        ra_rd[rp] += [pA, pB]
                        srcA, srcB = pA, pB
                    else:
                        srcA, srcB = eA, eB
                    slotA = pr * 130
                    oa_mm = P.op("tensor", lambda e, kt=kt, slotA=slotA,
                                 nkt=nkt, kb=kb, pvA=pvA:
                                 e.matmul(PB[pvA][0:65, :],
                                          VE[kt][:, slotA:slotA + 65],
                                          PAb[kb][:],
                                          start=(kt == 0), stop=(kt == nkt - 1)),
                                 waits=[srcA, v_done]
                                 + (pv_rd.get(pvA, []) if kt == 0 else []))
                    ob_mm = P.op("tensor", lambda e, kt=kt, slotA=slotA,
                                 nkt=nkt, kb=kb, pvB=pvB:
                                 e.matmul(PB[pvB][0:65, :],
                                          VE[kt][:, slotA + 65:slotA + 130],
                                          PBb[kb][:],
                                          start=(kt == 0), stop=(kt == nkt - 1)),
                                 waits=[srcB]
                                 + (pv_rd.get(pvB, []) if kt == 0 else []))
                    pab_rd[kb] = (oa_mm, ob_mm)
                sidx = pr * 4 + qc
                RCP = (rdr, xr[0])[p]
                rc1 = P.op("vector", lambda e, p=p, pvA=pvA, RCP=RCP:
                           e.reciprocal(
                               RCP[64:65, 0:512], PB[pvA][64:65, :]),
                           waits=[oa_mm] + d1_hist.get(p, []), small=True)
                rc2 = P.op("vector", lambda e, p=p, pvB=pvB, RCP=RCP:
                           e.reciprocal(
                               RCP[64:65, 512:1024], PB[pvB][64:65, :]),
                           waits=[ob_mm], small=True)
                shf = []
                for half, c0 in ((0, 0), (1, 512)):
                    for dq in range(2):
                        shf.append(P.op("vector",
                            lambda e, p=p, dq=dq, c0=c0, RCP=RCP:
                            e.stream_shuffle(
                                xo[p][dq * 32:(dq + 1) * 32, c0:c0 + 512],
                                RCP[64:96, c0:c0 + 512], [0] * 32),
                            waits=[rc1 if c0 == 0 else rc2]
                            + rb_rd.get(p, [])))
                d1_hist[p] = shf
                nA = P.op("vector", lambda e, pr=pr, qc=qc, p=p, pvA=pvA:
                          e.tensor_mul(
                              AC[pr][0:64, qc * 512:(qc + 1) * 512],
                              PB[pvA][0:64, :],
                              xo[p][0:64, 0:512]),
                          waits=shf[0:2] + [oa_mm, v_done])
                nB = P.op("vector", lambda e, pr=pr, qc=qc, p=p, pvB=pvB:
                          e.tensor_mul(
                              AC[pr][64:128, qc * 512:(qc + 1) * 512],
                              PB[pvB][0:64, :],
                              xo[p][0:64, 512:1024]),
                          waits=shf[2:4] + [ob_mm])
                pv_rd[pvA] = [rc1, nA]
                pv_rd[pvB] = [rc2, nB]
                rb_rd[p] = [nA, nB]
                norm_ops += [nA, nB]
        ac_done = norm_ops[-1]

        ld_bo = P.op("gpsimd", lambda e: e.dma_start(
            out=bob[:], in_=bass.AP(dbo, 0, [[0, 128], [1, D]])),
            waits=[ac_done] + d1_hist.get(0, []) + d1_hist.get(1, []),
            dma=sl_bo)
        lw4 = load_w(dwo, [ac_done])
        store_ops = []
        wo_ev = []
        for t in range(NT):
            b = t % 2
            lxr = P.op("sync", lambda e, b=b, t=t: e.dma_start(
                out=xr[b][:], in_=src_dram.ap()[t * 128:(t + 1) * 128, :]),
                waits=([store_ops[-2]] if len(store_ops) >= 2 else
                       d1_hist.get(0, []) + d1_hist.get(1, [])),
                dma=sl_xr[b])
            evs = []
            for dhf in range(2):
                bank = (t * 2 + dhf) % 2
                mm = None
                for dn in range(NDC):
                    mm = P.op("tensor", lambda e, dn=dn, t=t, dhf=dhf, bank=bank:
                              e.matmul(PB[bank][:],
                                       AC[dn][:, t * 128:(t + 1) * 128],
                                       WB[dn][:, dhf * 512:(dhf + 1) * 512],
                                       start=(dn == 0), stop=(dn == NDC - 1)),
                              waits=lw4 + [ac_done]
                              + ([wo_ev[-2]] if len(wo_ev) >= 2 else []))
                tb = P.op("vector", lambda e, bank=bank, dhf=dhf: e.tensor_add(
                    t1b[:], PB[bank][:], bob[:, dhf * 512:(dhf + 1) * 512]),
                    waits=[mm, ld_bo])
                ev = P.op("vector", lambda e, b=b, dhf=dhf: e.tensor_add(
                    xo[b][:, dhf * 512:(dhf + 1) * 512], t1b[:],
                    xr[b][:, dhf * 512:(dhf + 1) * 512]),
                    waits=[tb, lxr,
                           store_ops[-2] if len(store_ops) >= 2 else None])
                wo_ev.append(ev)
                evs.append(ev)
            store_ops.append(P.op("gpsimd", lambda e, b=b, t=t: e.dma_start(
                out=dst_dram.ap()[t * 128:(t + 1) * 128, :], in_=xo[b][:]),
                waits=evs, dma=sl_xo[b]))
        return store_ops

    if NPHASES == 1:
        cp = [P.op("gpsimd", lambda e, t=t: e.dma_start(
            out=dout.ap()[t * 128:(t + 1) * 128, :],
            in_=dx.ap()[t * 128:(t + 1) * 128, :]),
            waits=[db_sp] + sp1[-2:] + sp2[-2:], dma=P.slot("cp") if t == 0 else P.slots[-1])
            for t in range(NT)]
        c_stores = cp
    elif NPHASES == 2:
        st1 = ExitStack()
        c_stores = emit_ffn(st1, dx, dout, "fa")
        st1.close()
    elif NPHASES == 3:
        st1 = ExitStack()
        emit_ffn(st1, dx, dX1, "fa")
        P.barrier()
        st1.close()
        st2x = ExitStack()
        c_stores = emit_attn(st2x, dX1, dout)
        st2x.close()
    else:
        st1 = ExitStack()
        emit_ffn(st1, dx, dX1, "fa")
        P.barrier()
        st1.close()

        st2x = ExitStack()
        emit_attn(st2x, dX1, dX2)
        P.barrier()
        st2x.close()

        st3 = ExitStack()
        c_stores = emit_ffn(st3, dX2, dout, "fc")
        st3.close()

    if DEBUG:
        sl_dbg = P.slot("dbg")
        dbgsx = nc.dram_tensor("dbg_sx", [2052], F32, kind="ExternalOutput")
        dbgs2 = nc.dram_tensor("dbg_s2", [T], F32, kind="ExternalOutput")
        dbgw1 = nc.dram_tensor("dbg_w1", [128, HFF], BF16, kind="ExternalOutput")
        dbgdb = nc.dram_tensor("dbg_db", [HEADS, 2560], BF16, kind="ExternalOutput")
        last = P.last
        ws = [o for o in last.values() if o is not None]
        P.op("gpsimd", lambda e: e.dma_start(out=dbgsx.ap()[0:2052],
                                             in_=dSX.ap()[0:2052]),
             waits=ws, dma=sl_dbg)
        P.op("gpsimd", lambda e: e.dma_start(out=dbgs2.ap()[0:T],
                                             in_=dS2.ap()[0:T]), dma=sl_dbg)
        P.op("gpsimd", lambda e: e.dma_start(out=dbgw1.ap()[:, :],
                                             in_=dWQ1.ap()[0, :, :]), dma=sl_dbg)
        P.op("gpsimd", lambda e: e.dma_start(out=dbgdb.ap()[:, :],
                                             in_=dDB.ap()[:, :]), dma=sl_dbg)
        P.op("gpsimd", None, waits=[P.last["gpsimd"]])

    P.op("sync", None, waits=c_stores)

    counts = P.emit(stack)
    return nc, stack, counts


_CACHE = {}


def kernel(**ins):
    xfull = np.ascontiguousarray(np.asarray(ins["L0_attn_x"], dtype=np.float32))
    w1t = np.ascontiguousarray(np.asarray(ins["L0_ffn_w1"], np.float32).T)
    w2t = np.ascontiguousarray(np.asarray(ins["L0_ffn_w2"], np.float32).T)
    cw = np.ascontiguousarray(
        np.asarray(ins["L0_ffn_conv_w"], np.float32)[:, 0, :]
        .reshape(NHC, 128, 3).transpose(2, 1, 0))
    cb = np.ascontiguousarray(
        np.asarray(ins["L0_ffn_conv_b"], np.float32).reshape(NHC, 128).T)
    common = {
        "wq": np.ascontiguousarray(np.asarray(ins["L0_attn_Wq"], np.float32)),
        "wk": np.ascontiguousarray(np.asarray(ins["L0_attn_Wk"], np.float32)),
        "wv": np.ascontiguousarray(np.asarray(ins["L0_attn_Wv"], np.float32)),
        "wo": np.ascontiguousarray(np.asarray(ins["L0_attn_Wo"], np.float32)),
        "bo": np.ascontiguousarray(np.asarray(ins["L0_attn_bo"], np.float32)),
        "rel": np.ascontiguousarray(np.asarray(ins["L0_attn_rel_emb"], np.float32)),
        "w1t": w1t, "w2t": w2t, "cw": cw, "cb": cb,
    }
    if "nc" not in _CACHE:
        nc, stck, counts = build_program()
        _CACHE["nc"] = nc
        _CACHE["stack"] = stck
        print("bass counts:", counts, "sbuf_free:", nc.sbuf_bytes_remaining)
    nc = _CACHE["nc"]
    in_maps = [dict(common, x=np.ascontiguousarray(xfull[i])) for i in range(8)]
    res = run_bass_kernel_spmd(nc, in_maps, core_ids=list(range(8)))
    return np.stack([res.results[i]["out"] for i in range(8)], axis=0)


# version 20
# speedup vs baseline: 1.0253x; 1.0096x over previous
import math
from contextlib import ExitStack

import numpy as np
import ml_dtypes

import concourse.bass as bass
import concourse.mybir as mybir
from concourse.bass_utils import run_bass_kernel_spmd

F32 = mybir.dt.float32
BF16 = mybir.dt.bfloat16
AF = mybir.ActivationFunctionType
ALU = mybir.AluOpType
AX = mybir.AxisListType.X
NBF = ml_dtypes.bfloat16

T = 2048
D = 1024
HFF = 4096
HEADS = 16
NT = 16
NDC = 8
NHC = 32
NCH = 4
MAGIC = 12582912.0
CEXP = 12.0

ENGS = ("tensor", "vector", "scalar", "gpsimd", "sync")


class _Slot:

    def __init__(self, name):
        self.name = name
        self.count = 0
        self.sem = None


_SEQ = [0]


class _Op:
    __slots__ = ("eng", "fn", "waits", "amt", "marked", "val", "slot",
                 "small", "seq")

    def __init__(self, eng, fn, waits, amt, slot, small=False):
        self.eng, self.fn, self.waits, self.amt = eng, fn, waits, amt
        self.marked = False
        self.val = None
        self.slot = slot
        self.small = small
        _SEQ[0] += 1
        self.seq = _SEQ[0]


class Prog:

    def __init__(self, nc):
        self.nc = nc
        self.q = {e: [] for e in ENGS}
        self.last = {e: None for e in ENGS}
        self.slots = []
        self.slot_last = {}

    def slot(self, name):
        s = _Slot(name)
        self.slots.append(s)
        return s

    def op(self, eng, fn, waits=(), dma=None, small=False):
        ws = []
        for w in waits:
            if w is None:
                continue
            assert isinstance(w, _Op), f"bad wait {w!r}"
            if w.eng == eng and w.slot is None:
                continue
            if w not in ws:
                ws.append(w)
        o = _Op(eng, fn, ws, 16 if dma is not None else 1, dma, small=small)
        if dma is not None:
            o.marked = True
            dma.count += 16
            o.val = dma.count
            self.slot_last[id(dma)] = o
        self.q[eng].append(o)
        self.last[eng] = o
        return o

    def barrier(self):
        lasts = {e: self.last[e] for e in ENGS if self.last[e] is not None}
        slot_lasts = list(self.slot_last.values())
        for e in ENGS:
            ws = [o for e2, o in lasts.items() if e2 != e and o.slot is None]
            ws += slot_lasts
            if ws:
                self.op(e, None, waits=ws)

    CHAIN = ()

    def emit(self, stack):
        nc = self.nc
        for e in self.CHAIN:
            prev = None
            for o in self.q[e]:
                if o.slot is not None or o.fn is None:
                    continue
                if prev is not None and prev not in o.waits:
                    o.waits.append(prev)
                prev = o
        for e in ENGS:
            prev = None
            for o in self.q[e]:
                if o.slot is not None or o.fn is None:
                    continue
                if prev is not None and prev.small and prev not in o.waits:
                    o.waits.append(prev)
                prev = o
        for e in ENGS:
            for o in self.q[e]:
                for w in o.waits:
                    w.marked = True
        sems = {e: stack.enter_context(nc.semaphore(f"sem_{e}")) for e in ENGS}
        for s in self.slots:
            s.sem = stack.enter_context(nc.semaphore(f"slt_{s.name}"))
        for e in ENGS:
            c = 0
            for o in self.q[e]:
                if o.slot is None and o.marked:
                    c += o.amt
                    o.val = c
        spacer = stack.enter_context(nc.sbuf_tensor("spacer_junk", [128, 16], F32))
        block = stack.enter_context(nc.Block())

        def sem_of(o):
            return o.slot.sem if o.slot is not None else sems[o.eng]

        def key_of(o):
            return id(o.slot) if o.slot is not None else o.eng

        know = {}
        for o in sorted((o for q in self.q.values() for o in q),
                        key=lambda o: o.seq):
            k = {}
            for w in o.waits:
                kw = know.get(w.seq)
                if kw:
                    for kk, vv in kw.items():
                        if k.get(kk, -1) < vv:
                            k[kk] = vv
                kk = key_of(w)
                if k.get(kk, -1) < w.val:
                    k[kk] = w.val
            if o.marked and o.val is not None:
                kk = key_of(o)
                if k.get(kk, -1) < o.val:
                    k[kk] = o.val
            know[o.seq] = k

        def mk(e):
            def body(eng):
                maxw = {}
                for o in self.q[e]:
                    need = {}
                    for w in o.waits:
                        k = key_of(w)
                        if need.get(k, -1) < w.val:
                            need[k] = w.val
                    for w in sorted(o.waits, key=lambda w: -(w.val or 0)):
                        k = key_of(w)
                        if need.get(k, -1) != w.val:
                            continue
                        del need[k]
                        if maxw.get(k, -1) >= w.val:
                            continue
                        eng.wait_ge(sem_of(w), w.val)
                        if maxw.get(k, -1) < w.val:
                            maxw[k] = w.val
                        for kk, vv in know[w.seq].items():
                            if maxw.get(kk, -1) < vv:
                                maxw[kk] = vv
                    if o.fn is None:
                        if o.marked and o.slot is None:
                            eng.nop().then_inc(sems[e], o.amt)
                        continue
                    inst = o.fn(eng)
                    if o.marked:
                        inst.then_inc(sem_of(o), o.amt)
            return body

        block.tensor(mk("tensor"))
        block.vector(mk("vector"))
        block.scalar(mk("scalar"))
        block.gpsimd(mk("gpsimd"))
        block.sync(mk("sync"))
        return {e: len(self.q[e]) for e in ENGS}


def _oh_const():
    n = np.arange(T)
    val = 16 + (np.log(np.maximum(n, 1).astype(np.float32) / np.float32(16.0))
                / np.float32(math.log(8.0)) * np.float32(16.0)).astype(np.int32)
    val = np.minimum(val, 31)
    buckets = np.where(n < 16, n, val)
    oh = np.zeros((33, 2560), np.float32)
    oh[buckets[::-1], np.arange(T)] = 1.0
    oh[31, :T] -= 1.0
    oh[32, T:] = 1.0
    return oh


NPHASES = 4
DEBUG = False


def build_program():
    nc = bass.Bass("TRN2", target_bir_lowering=False,
                   detect_race_conditions=False)
    stack = ExitStack()

    dx = nc.dram_tensor("x", [T, D], F32, kind="ExternalInput")
    dwq = nc.dram_tensor("wq", [D, D], F32, kind="ExternalInput")
    dwk = nc.dram_tensor("wk", [D, D], F32, kind="ExternalInput")
    dwv = nc.dram_tensor("wv", [D, D], F32, kind="ExternalInput")
    dwo = nc.dram_tensor("wo", [D, D], F32, kind="ExternalInput")
    dbo = nc.dram_tensor("bo", [D], F32, kind="ExternalInput")
    drel = nc.dram_tensor("rel", [32, HEADS], F32, kind="ExternalInput")
    dw1t = nc.dram_tensor("w1t", [D, HFF], F32, kind="ExternalInput")
    dw2t = nc.dram_tensor("w2t", [HFF, D], F32, kind="ExternalInput")
    dcw = nc.dram_tensor("cw", [3, 128, NHC], F32, kind="ExternalInput")
    dcb = nc.dram_tensor("cb", [128, NHC], F32, kind="ExternalInput")
    dout = nc.dram_tensor("out", [T, D], F32, kind="ExternalOutput")

    dX1 = nc.dram_tensor("X1", [T, D], F32)
    dX2 = nc.dram_tensor("X2", [T, D], F32)
    dWQ1 = nc.dram_tensor("WQ1", [NDC, 128, HFF], BF16)
    dWQ2 = nc.dram_tensor("WQ2", [NHC, 128, D], BF16)
    dDB = nc.dram_tensor("DB", [HEADS, 2560], BF16)
    dSX = nc.dram_tensor("SX", [2052], F32)
    dS2 = nc.dram_tensor("S2", [T], F32)
    dRD = nc.dram_tensor("RD", [32, 1024], F32)

    identb = nc.inline_tensor(np.eye(128, dtype=NBF), name="identb")
    identf = nc.inline_tensor(np.eye(128, dtype=np.float32), name="identf")
    onescol = nc.inline_tensor(np.ones((128, 1), np.float32), name="onescol")
    onesrow = nc.inline_tensor(np.ones((1, 128), np.float32), name="onesrow")
    doh = nc.inline_tensor(_oh_const(), name="ohc")

    P = Prog(nc)
    SB = lambda name, shape, dt_, st=stack: st.enter_context(
        nc.sbuf_tensor(name, shape, dt_))
    PS = lambda name, shape, dt_, st=stack: st.enter_context(
        nc.psum_tensor(name, shape, dt_))

    IDB = SB("idb", [128, 128], BF16)
    IDF = SB("idf", [128, 128], F32)
    ONC = SB("onc", [128, 1], F32)
    ONR = SB("onr", [1, 128], F32)
    TAP = [SB(f"tap{k}", [128, NHC], F32) for k in range(3)]
    CBT = SB("cbt", [128, NHC], F32)
    BC12 = SB("bc12", [128, 1], F32)
    S125 = SB("s125", [128, 1], F32)
    SW1I = SB("sw1i", [128, 1], F32)
    SW2I = SB("sw2i", [128, 1], F32)
    SW1V = SB("sw1v", [128, 1], F32)
    SW2V = SB("sw2v", [128, 1], F32)
    PB = [PS(f"pbank{i}", [128, 512], F32) for i in range(8)]

    sl_const = P.slot("consts")
    P.op("sync", lambda e: e.dma_start(out=IDB[:], in_=identb.ap()[:, :]), dma=sl_const)
    P.op("sync", lambda e: e.dma_start(out=IDF[:], in_=identf.ap()[:, :]), dma=sl_const)
    P.op("sync", lambda e: e.dma_start(out=ONC[:], in_=onescol.ap()[:, :]), dma=sl_const)
    P.op("sync", lambda e: e.dma_start(out=ONR[:], in_=onesrow.ap()[:, :]), dma=sl_const)
    P.op("sync", lambda e: e.dma_start(out=CBT[:], in_=dcb.ap()[:, :]), dma=sl_const)
    ld_all = None
    for k in range(3):
        ld_all = P.op("sync", lambda e, k=k: e.dma_start(
            out=TAP[k][:], in_=dcw.ap()[k, :, :]), dma=sl_const)
    ld_idb = ld_idf = ld_onc = ld_onr = ld_cbt = ld_all
    ld_tap = [ld_all, ld_all, ld_all]
    P.op("vector", lambda e: e.memset(BC12[:], -CEXP))
    P.op("vector", lambda e: e.memset(S125[:], 0.125))

    def emit_wquant(st, src, n_chunks, width, ddst, swv, swiv, extra, nm):
        wbuf = [SB(f"{nm}w{i}", [128, width], F32, st) for i in range(2)]
        qbuf = [SB(f"{nm}q{i}", [128, width], BF16, st) for i in range(2)]
        jabs = SB(f"{nm}ja", [128, width], F32, st)
        jrnd = SB(f"{nm}jr", [128, width], F32, st)
        acc = SB(f"{nm}ac", [128, n_chunks], F32, st)
        sums = SB(f"{nm}su", [128, 1], F32, st)
        sc = SB(f"{nm}sc", [1, 4], F32, st)
        sl_w = [P.slot(f"{nm}l{i}") for i in range(2)]
        sl_sp = [P.slot(f"{nm}sp{i}") for i in range(2)]
        absops = []
        for c in range(n_chunks):
            b = c % 2
            war = [absops[c - 2]] if c >= 2 else []
            ld = P.op("sync", lambda e, b=b, c=c: e.dma_start(
                out=wbuf[b][:], in_=src.ap()[c * 128:(c + 1) * 128, :]),
                waits=war, dma=sl_w[b])
            absops.append(P.op("scalar", lambda e, b=b, c=c: e.activation(
                jabs[:], wbuf[b][:], AF.Abs, accum_out=acc[:, c:c + 1]),
                waits=[ld]))
        rs = P.op("vector", lambda e, n_chunks=n_chunks: e.reduce_sum(
            sums[:], acc[:, 0:n_chunks], axis=AX), waits=absops)
        mt = P.op("tensor", lambda e: e.matmul(PB[0][0:1, 0:1], ONC[:], sums[:],
                                               start=True, stop=True),
                  waits=[rs, ld_onc])
        v1 = P.op("vector", lambda e, n_chunks=n_chunks, width=width:
                  e.tensor_scalar(sc[:, 0:1], PB[0][0:1, 0:1],
                                  1.0 / (n_chunks * 128 * width), 1e-5,
                                  ALU.mult, ALU.max), waits=[mt], small=True)
        v2 = P.op("vector", lambda e: e.reciprocal(sc[:, 1:2], sc[:, 0:1]),
                  small=True)
        v3 = P.op("vector", lambda e, extra=extra: e.tensor_scalar(
            sc[:, 2:3], sc[:, 0:1], extra, None, ALU.mult), small=True)
        mb = P.op("tensor", lambda e: e.matmul(
            PB[1][:, 0:2], ONR[:], sc[:, 1:3], start=True, stop=True),
            waits=[v2, v3, ld_onr])
        b1 = P.op("vector", lambda e, swv=swv: e.tensor_copy(
            swv[:], PB[1][:, 0:1]), waits=[mb], small=True)
        b2 = P.op("vector", lambda e, swiv=swiv: e.tensor_copy(
            swiv[:], PB[1][:, 1:2]), waits=[mb], small=True)
        spills = []
        rounds = []
        for c in range(n_chunks):
            b = c % 2
            war = [rounds[c - 2]] if c >= 2 else [absops[-1]]
            ld = P.op("sync", lambda e, b=b, c=c: e.dma_start(
                out=wbuf[b][:], in_=src.ap()[c * 128:(c + 1) * 128, :]),
                waits=war, dma=sl_w[b])
            r1 = P.op("vector", lambda e, b=b: e.tensor_scalar(
                jrnd[:], wbuf[b][:], swv[:], None, ALU.mult), waits=[b1, ld])
            rounds.append(r1)
            P.op("vector", lambda e: e.tensor_scalar(
                jrnd[:], jrnd[:], MAGIC, MAGIC, ALU.add, ALU.subtract))
            r3 = P.op("vector", lambda e, b=b: e.tensor_scalar(
                qbuf[b][:], jrnd[:], 1.0, -1.0, ALU.min, ALU.max),
                waits=[spills[c - 2]] if c >= 2 else [])
            spills.append(P.op("gpsimd", lambda e, b=b, c=c: e.dma_start(
                out=ddst.ap()[c, :, :], in_=qbuf[b][:]), waits=[r3],
                dma=sl_sp[b]))
        return spills, b2

    wst = ExitStack()
    sp1, _ = emit_wquant(wst, dw1t, NDC, HFF, dWQ1, SW1V, SW1I, 1.0 / 127.0, "w1")
    sp2, _ = emit_wquant(wst, dw2t, NHC, D, dWQ2, SW2V, SW2I, 0.5 / 127.0, "w2")

    relx = SB("relx", [33, HEADS], F32, wst)
    ohs = SB("ohs", [33, 2560], F32, wst)
    dbs = SB("dbs", [HEADS, 2560], BF16, wst)
    sl_dbin = P.slot("dbin")
    sl_dbsp = P.slot("dbsp")
    r1 = P.op("sync", lambda e: e.dma_start(out=relx[0:32, :], in_=drel.ap()[:, :]), dma=sl_dbin)
    r2 = P.op("vector", lambda e: e.memset(relx[32:33, :], -30000.0))
    r3 = P.op("sync", lambda e: e.dma_start(out=ohs[:], in_=doh.ap()[:, :]), dma=sl_dbin)
    dbev = []
    for i in range(5):
        mm = P.op("tensor", lambda e, i=i: e.matmul(
            PB[2 + i][0:HEADS, :], relx[:], ohs[:, i * 512:(i + 1) * 512],
            start=True, stop=True), waits=[r1, r2, r3])
        dbev.append(P.op("scalar", lambda e, i=i: e.activation(
            dbs[:, i * 512:(i + 1) * 512], PB[2 + i][0:HEADS, :], AF.Exp),
            waits=[mm]))
    db_sp = P.op("gpsimd", lambda e: e.dma_start(out=dDB.ap()[:, :], in_=dbs[:]),
                 waits=dbev, dma=sl_dbsp)
    P.barrier()
    wst.close()

    def emit_ln(SBL, src_dram, quant, XQdst, Ydst, srow, zrow, sx_sp_dram, jnk,
                tag):
        xt = [SBL(f"xt{i}", [128, D], F32) for i in range(4)]
        yt = [SBL(f"yt{i}", [128, D], F32) for i in range(2)] if quant else None
        qt = SBL("qt", [128, D], F32) if quant else None
        qb = [SBL(f"qb{i}", [128, D], BF16) for i in range(2)]
        stats = SBL("sta", [128, 16], F32)
        statq = SBL("stq", [128, 16], F32)
        sl_xt = [P.slot(f"{tag}xt{i}") for i in range(4)]
        sl_sx = P.slot(f"{tag}sxs") if quant else None
        prev_cons = {}
        prev_q1 = {}
        prev_tr = {}
        prev_d3 = None
        tr_list = []
        last_sx = None
        tr_flip = [0]
        zs = None
        zm = []
        if quant:
            z4 = P.op("vector", lambda e: e.memset(zrow[:], 0.0))
            z1 = P.op("gpsimd", lambda e: e.dma_start(
                out=sx_sp_dram.ap()[0:1], in_=zrow[0:1, 0:1]),
                waits=[z4], dma=sl_sx)
            zs = P.op("gpsimd", lambda e: e.dma_start(
                out=sx_sp_dram.ap()[2049:2052], in_=zrow[0:1, 0:3]),
                waits=[z4], dma=sl_sx)
            zm = [P.op("gpsimd", lambda e, d=d: e.memset(XQdst[d][:, 0:1], 0))
                  for d in range(NDC)]
            zm += [P.op("gpsimd", lambda e, d=d: e.memset(
                XQdst[d][:, 2049:2052], 0)) for d in range(NDC)]
        for tc in range(NCH):
            red, sq = [], []
            for j in range(4):
                tt = tc * 4 + j
                ld = P.op("sync", lambda e, j=j, tt=tt: e.dma_start(
                    out=xt[j][:], in_=src_dram.ap()[tt * 128:(tt + 1) * 128, :]),
                    waits=[prev_cons.get(j)], dma=sl_xt[j])
                red.append(P.op("vector", lambda e, j=j: e.reduce_sum(
                    stats[:, j:j + 1], xt[j][:], axis=AX),
                    waits=[ld, prev_d3], small=True))
                sq.append(P.op("scalar", lambda e, j=j: e.activation(
                    jnk[:], xt[j][:], AF.Square,
                    accum_out=stats[:, 4 + j:5 + j]), waits=[ld, prev_d3]))
            P.op("vector", lambda e: e.tensor_scalar(
                stats[:, 8:12], stats[:, 0:4], -1.0 / D, None, ALU.mult),
                waits=red, small=True)
            P.op("vector", lambda e: e.tensor_scalar(
                stats[:, 12:16], stats[:, 4:8], 1.0 / D, None, ALU.mult),
                waits=sq, small=True)
            P.op("vector", lambda e: e.tensor_tensor(
                statq[:, 0:4], stats[:, 8:12], stats[:, 8:12], ALU.mult),
                small=True)
            P.op("vector", lambda e: e.tensor_tensor(
                statq[:, 4:8], stats[:, 12:16], statq[:, 0:4], ALU.subtract),
                small=True)
            P.op("vector", lambda e: e.tensor_scalar(
                statq[:, 8:12], statq[:, 4:8], 1e-5, None, ALU.add),
                small=True)
            c5 = P.op("vector", lambda e: e.reciprocal(
                statq[:, 12:16], statq[:, 8:12]), small=True)
            c6 = P.op("scalar", lambda e: e.sqrt(stats[:, 0:4],
                                                 statq[:, 12:16]), waits=[c5])
            P.op("vector", lambda e: e.tensor_tensor(
                stats[:, 4:8], stats[:, 0:4], stats[:, 0:4], ALU.mult),
                waits=[c6], small=True)
            P.op("vector", lambda e: e.tensor_tensor(
                statq[:, 0:4], stats[:, 4:8], statq[:, 8:12], ALU.mult),
                small=True)
            P.op("vector", lambda e: e.tensor_scalar(
                statq[:, 4:8], statq[:, 0:4], -0.5, 1.5, ALU.mult, ALU.add),
                small=True)
            c10 = P.op("vector", lambda e: e.tensor_tensor(
                stats[:, 12:16], stats[:, 0:4], statq[:, 4:8], ALU.mult),
                small=True)
            c11 = P.op("vector", lambda e: e.tensor_tensor(
                statq[:, 8:12], stats[:, 8:12], stats[:, 12:16], ALU.mult),
                small=True)
            u2s = []
            for j in range(4):
                tt = tc * 4 + j
                b = tt % 2
                if quant:
                    ya = P.op("scalar", lambda e, b=b, j=j: e.activation(
                        yt[b][:], xt[j][:], AF.Identity, bias=statq[:, 8 + j:9 + j],
                        scale=stats[:, 12 + j:13 + j]),
                        waits=[c10, c11, prev_q1.get(b)])
                    mxo = P.op("vector", lambda e, b=b, j=j: e.reduce_max(
                        statq[:, 12 + j:13 + j], yt[b][:], axis=AX,
                        apply_absolute_value=True), waits=[ya], small=True)
                    u0 = P.op("vector", lambda e, j=j: e.tensor_scalar(
                        stats[:, 0 + j:1 + j], statq[:, 12 + j:13 + j],
                        1e-5, None, ALU.max), waits=[mxo], small=True)
                    u1 = P.op("vector", lambda e, j=j: e.reciprocal(
                        stats[:, 4 + j:5 + j], stats[:, 0 + j:1 + j]),
                        small=True)
                    u2 = P.op("vector", lambda e, j=j: e.tensor_scalar(
                        statq[:, 0 + j:1 + j], stats[:, 4 + j:5 + j],
                        127.0, None, ALU.mult), small=True)
                    u2s.append(u2)
                    q1 = P.op("vector", lambda e, b=b, j=j: e.tensor_scalar(
                        qt[:], yt[b][:], statq[:, 0 + j:1 + j], MAGIC,
                        ALU.mult, ALU.add), waits=[u2])
                    prev_q1[b] = q1
                    q2 = P.op("vector", lambda e, b=b: e.tensor_scalar(
                        qb[b][:], qt[:], MAGIC, None, ALU.subtract),
                        waits=[prev_tr.get(b)])
                    prev_cons[j] = q2
                    src_t = qb[b]
                else:
                    ya = P.op("scalar", lambda e, b=b, j=j: e.activation(
                        qb[b][:], xt[j][:], AF.Identity, bias=statq[:, 8 + j:9 + j],
                        scale=stats[:, 12 + j:13 + j]),
                        waits=[c10, c11, prev_tr.get(b)])
                    prev_cons[j] = ya
                    q2 = ya
                    src_t = qb[b]
                trs = None
                for dc in range(NDC):
                    f = tr_flip[0] % 2
                    pb_ap = PB[5 + f].ap()[:, 0:64].bitcast(BF16)
                    tr1 = P.op("tensor",
                               lambda e, dc=dc, pb_ap=pb_ap, src_t=src_t:
                               e.transpose(pb_ap,
                                           src_t[:, dc * 128:(dc + 1) * 128],
                                           IDB[:]),
                               waits=[q2, ld_idb]
                               + ([tr_list[-2]] if len(tr_list) >= 2 else []))
                    if quant:
                        tr2 = P.op("vector",
                                   lambda e, dc=dc, tt=tt, pb_ap=pb_ap:
                                   e.tensor_copy(
                                       XQdst[dc][:, 1 + tt * 128:1 + (tt + 1) * 128],
                                       pb_ap), waits=[tr1] + zm)
                    else:
                        tr2 = P.op("vector",
                                   lambda e, dc=dc, tt=tt, pb_ap=pb_ap:
                                   e.tensor_copy(
                                       Ydst[dc][:, tt * 128:(tt + 1) * 128],
                                       pb_ap), waits=[tr1])
                    tr_list.append(tr2)
                    trs = tr2
                    tr_flip[0] += 1
                prev_tr[b] = trs
            if quant:
                d3 = P.op("vector", lambda e: e.tensor_scalar(
                    stats[:, 4:8], stats[:, 0:4], SW1I[:], None, ALU.mult),
                    waits=u2s, small=True)
                trx = P.op("tensor", lambda e: e.transpose(
                    PB[7][0:4, 0:128], stats[:, 4:8], IDF[:]),
                    waits=[d3, ld_idf, last_sx])
                prev_d3 = trx
                cp4 = P.op("vector", lambda e: e.tensor_copy(
                    srow[:], PB[7][0:4, 0:128]), waits=[trx, last_sx])
                last_sx = P.op("gpsimd", lambda e, tc=tc: e.dma_start(
                    out=sx_sp_dram.ap()[1 + tc * 512:1 + (tc + 1) * 512],
                    in_=srow[:]), waits=[cp4, zs], dma=sl_sx)
        sx_sp = last_sx
        return tr_list[-1], sx_sp

    def emit_ffn(st, src_dram, dst_dram, tag):
        SBL = lambda n, sh, dt_: SB(f"{tag}{n}", sh, dt_, st)
        XQ = [SBL(f"xq{d}", [128, 2052], BF16) for d in range(NDC)]
        srow = SBL("srow", [4, 128], F32)
        zrow = SBL("zrow", [1, 4], F32)
        GB = [SBL(f"gb{h}", [128, 515], F32) for h in range(NHC)]
        CT = [SBL(f"ct{h}", [128, 3], F32) for h in range(NHC)]
        W1S = [[SBL(f"w1s{b}_{d}", [128, 512], BF16) for d in range(NDC)]
               for b in range(2)]
        W2S = [[SBL(f"w2s{b}_{j}", [128, D], BF16) for j in range(4)]
               for b in range(3)]
        sxb = [SBL(f"sxb{i}", [128, 512], F32) for i in range(2)]
        sxp = SBL("sxp", [128, 2], F32)
        s2b = SBL("s2b", [128, 512], F32)
        cva = SBL("cva", [128, 512], F32)
        cvb = SBL("cvb", [128, 512], F32)
        tqt = SBL("tq", [128, 512], F32)
        MX = SBL("mx", [128, 512], F32)
        MN = SBL("mn", [128, 512], F32)
        st2 = SBL("st2", [128, 16], F32)
        s2s4 = SBL("s2s4", [32, 128], F32)
        xr = [SBL(f"xr{i}", [128, D], F32) for i in range(2)]
        xo = [SBL(f"xo{i}", [128, D], F32) for i in range(2)]
        hqv = [GB[h].ap()[:, 0:256].bitcast(BF16) for h in range(NHC)]
        for h in range(NHC):
            assert tuple(hqv[h].shape) == (128, 512), hqv[h].shape

        ct0 = [P.op("vector", lambda e, h=h: e.memset(CT[h][:, 0:1], 0.0))
               for h in range(NHC)]
        s2z = P.op("vector", lambda e: e.memset(s2s4[:], 0.0))
        sl_sxb = P.slot(f"{tag}sxb")
        sl_sxp = P.slot(f"{tag}sxp")
        sl_w1 = [P.slot(f"{tag}w1{i}") for i in range(3)]
        sl_w2 = [P.slot(f"{tag}w2{i}") for i in range(3)]
        sl_s2s = P.slot(f"{tag}s2s")
        sl_s2b = P.slot(f"{tag}s2b")
        sl_xr = [P.slot(f"{tag}xr{i}") for i in range(2)]
        sl_xo = [P.slot(f"{tag}xo{i}") for i in range(2)]
        xq_done, sx_sp = emit_ln(SBL, src_dram, True, XQ, None, srow, zrow,
                                 dSX, xo[0], tag)

        geighth = [0]
        g2idx = [0]
        mm2_of_gen = {}
        last_mm1_of_eighth = {}
        prev_sxl = None
        prev_s2l = None
        prev_mm2ev = {k: None for k in range(8)}
        hq_ops = {}
        preev = []
        store_ops = []
        prev_trh = []
        prev_s2copy = None
        psxl = {}
        for c in range(NCH):
            sxl = P.op("gpsimd", lambda e, c=c: e.dma_start(
                out=sxb[c % 2][:],
                in_=bass.AP(dSX, 512 * c + 3, [[0, 128], [1, 512]])),
                waits=[sx_sp, psxl.get(c % 2)], dma=sl_sxb)
            if c == 0:
                sxpl = P.op("gpsimd", lambda e: e.dma_start(
                    out=sxp[:], in_=bass.AP(dSX, 1, [[0, 128], [1, 2]])),
                    waits=[sx_sp], dma=sl_sxp)
            mm1ev = {}
            mx_folds = []
            mn_folds = []
            mzero = P.op("vector", lambda e: e.memset(MX[:], 0.0),
                         waits=prev_trh)
            nzero = P.op("vector", lambda e: e.memset(MN[:], 0.0),
                         waits=prev_trh)
            conv_sil = []
            cur_lds = None
            cur_gb = 0
            for hc in range(NHC):
                eh = hc // 4
                if hc % 4 == 0:
                    g = geighth[0]
                    cur_gb = g % 2
                    war = [last_mm1_of_eighth.get(g - 2)]
                    cur_lds = [P.op("sync" if dn < 4 else "scalar",
                                    lambda e, gb=cur_gb, dn=dn, eh=eh:
                                    e.dma_start(
                                        out=W1S[gb][dn][:],
                                        in_=dWQ1.ap()[dn, :, eh * 512:(eh + 1) * 512]),
                                    waits=war + sp1[-2:],
                                    dma=sl_w1[cur_gb])
                               for dn in range(NDC)]
                    geighth[0] += 1
                bank = hc % 2
                if c == 0:
                    pmm = None
                    pwar = [preev[hc - 2]] if hc >= 2 else []
                    for dn in range(NDC):
                        pmm = P.op("tensor",
                                   lambda e, dn=dn, hc=hc, gb=cur_gb:
                                   e.matmul(
                                       PB[2 + hc % 2][:, 0:2],
                                       W1S[gb][dn][:, (hc % 4) * 128:(hc % 4 + 1) * 128],
                                       XQ[dn][:, 1:3],
                                       start=(dn == 0), stop=(dn == NDC - 1)),
                                   waits=cur_lds + [xq_done] + pwar)
                    preev.append(P.op("vector", lambda e, hc=hc: e.tensor_tensor(
                        CT[hc][:, 1:3],
                        PB[2 + hc % 2][:, 0:2],
                        sxp[:], ALU.mult), waits=[pmm, sxpl], small=True))
                mm = None
                for dn in range(NDC):
                    mm = P.op("tensor",
                              lambda e, dn=dn, hc=hc, bank=bank, c=c, gb=cur_gb:
                              e.matmul(
                                  PB[bank][:],
                                  W1S[gb][dn][:, (hc % 4) * 128:(hc % 4 + 1) * 128],
                                  XQ[dn][:, 512 * c + 3:512 * c + 515],
                                  start=(dn == 0), stop=(dn == NDC - 1)),
                              waits=cur_lds + [xq_done, mm1ev.get(hc - 2),
                                               prev_mm2ev[bank]])
                last_mm1_of_eighth[geighth[0] - 1] = mm
                war_gb = hq_ops.get((c - 1, hc))
                ev = P.op("vector", lambda e, hc=hc, bank=bank, c=c:
                          e.tensor_tensor(
                    GB[hc][:, 3:515], PB[bank][:], sxb[c % 2][:], ALU.mult),
                    waits=[mm, sxl, war_gb])
                mm1ev[hc] = ev
                cin = P.op("vector", lambda e, hc=hc: e.tensor_copy(
                    GB[hc][:, 0:3], CT[hc][:]),
                    waits=[ct0[hc], war_gb] + ([preev[hc]] if c == 0 else []),
                    small=True)
                cout = P.op("vector", lambda e, hc=hc: e.tensor_copy(
                    CT[hc][:], GB[hc][:, 512:515]), waits=[ev], small=True)
                cc = P.op("scalar", lambda e, hc=hc: e.activation(
                    cva[:], GB[hc][:, 1:513], AF.Identity,
                    bias=CBT[:, hc:hc + 1], scale=TAP[1][:, hc:hc + 1]),
                    waits=[ev, cin, ld_cbt, ld_tap[1]])
                s1 = P.op("vector", lambda e, hc=hc:
                          e.scalar_tensor_tensor(
                              cvb[:], GB[hc][:, 0:512], TAP[0][:, hc:hc + 1],
                              cva[:], ALU.mult, ALU.add),
                          waits=[cc, ld_tap[0]])
                s2_ = P.op("vector", lambda e, hc=hc:
                           e.scalar_tensor_tensor(
                               cva[:], GB[hc][:, 2:514], TAP[2][:, hc:hc + 1],
                               cvb[:], ALU.mult, ALU.add),
                           waits=[ld_tap[2]])
                sil = P.op("scalar", lambda e, hc=hc: e.activation(
                    GB[hc][:, 0:512], cva[:], AF.Silu), waits=[s2_, cout])
                conv_sil.append(sil)
                mx_folds.append(P.op("vector", lambda e, hc=hc: e.tensor_tensor(
                    MX[:], MX[:], GB[hc][:, 0:512], ALU.max),
                    waits=[sil, mzero]))
                mn_folds.append(P.op("vector", lambda e, hc=hc: e.tensor_tensor(
                    MN[:], MN[:], GB[hc][:, 0:512], ALU.min),
                    waits=[sil, nzero]))
            trh_reads = []
            for i in range(4):
                trh = P.op("tensor", lambda e, i=i: e.transpose(
                    PB[3][:, 0:128], MX[:, i * 128:(i + 1) * 128], IDF[:]),
                    waits=mx_folds + [ld_idf, prev_mm2ev[3]]
                    + ([trh_reads[-1]] if trh_reads else []))
                rd_ = P.op("vector", lambda e, i=i: e.reduce_max(
                    st2[:, 0 + i:1 + i], PB[3][:, 0:128], axis=AX), waits=[trh],
                    small=True)
                trn = P.op("tensor", lambda e, i=i: e.transpose(
                    PB[3][:, 0:128], MN[:, i * 128:(i + 1) * 128], IDF[:]),
                    waits=mn_folds + [rd_])
                rdn = P.op("vector", lambda e, i=i: e.tensor_reduce(
                    st2[:, 12 + i:13 + i], PB[3][:, 0:128], AX, ALU.min),
                    waits=[trn], small=True)
                trh_reads.append(rdn)
            negn = P.op("vector", lambda e: e.tensor_scalar(
                st2[:, 4:8], st2[:, 12:16], -1.0, None, ALU.mult),
                waits=trh_reads, small=True)
            pmax = P.op("vector", lambda e: e.tensor_tensor(
                st2[:, 8:12], st2[:, 0:4], st2[:, 4:8], ALU.max), small=True)
            prev_trh = trh_reads + [pmax]
            e0 = P.op("vector", lambda e: e.tensor_scalar(
                st2[:, 4:8], st2[:, 8:12], 1e-5, None, ALU.max), waits=[pmax],
                small=True)
            P.op("vector", lambda e: e.reciprocal(st2[:, 0:4], st2[:, 4:8]),
                 small=True)
            e2 = P.op("vector", lambda e: e.tensor_scalar(
                st2[:, 12:16], st2[:, 0:4], 127.0, None, ALU.mult), small=True)
            e3 = P.op("vector", lambda e: e.tensor_scalar(
                st2[:, 8:12], st2[:, 4:8], SW2I[:], None, ALU.mult), small=True)
            trs2 = P.op("tensor", lambda e: e.transpose(
                PB[7][0:4, 0:128], st2[:, 12:16], IDF[:]),
                waits=[e2, prev_s2copy, ld_idf, prev_mm2ev[7]])
            s2st = P.op("vector", lambda e: e.tensor_copy(
                s2s4[0:4, :], PB[7][0:4, 0:128]), waits=[trs2, s2z, prev_s2copy])
            s2shf = []
            for j in range(4):
                for dq in range(4):
                    s2shf.append(P.op("vector",
                        lambda e, j=j, dq=dq: e.stream_shuffle(
                            s2b[dq * 32:(dq + 1) * 32, j * 128:(j + 1) * 128],
                            s2s4[0:32, :], [j] * 32),
                        waits=[s2st, prev_s2l]))
            prev_s2copy = s2shf[-1]
            s2l = P.op("vector", lambda e: e.nop(), waits=s2shf)
            hq_list = []
            for hc in range(NHC):
                t1 = P.op("vector", lambda e, hc=hc: e.tensor_tensor(
                    tqt[:], GB[hc][:, 0:512], s2b[:], ALU.mult),
                    waits=[s2l, mx_folds[hc]])
                t2 = P.op("vector", lambda e, hc=hc: e.tensor_scalar(
                    hqv[hc], tqt[:], MAGIC, MAGIC, ALU.add, ALU.subtract))
                hq_ops[(c, hc)] = t2
                hq_list.append(t2)
            mm2 = {k: None for k in range(8)}
            for g in range(8):
                gg = g2idx[0]
                gb2 = gg % 3
                war = mm2_of_gen.get(gg - 3, [])
                l2 = [P.op("sync" if j < 2 else "scalar",
                           lambda e, gb2=gb2, j=j, g=g: e.dma_start(
                    out=W2S[gb2][j][:], in_=dWQ2.ap()[g * 4 + j, :, :]),
                    waits=war + sp2[-2:], dma=sl_w2[gb2]) for j in range(4)]
                mm2_of_gen[gg] = []
                g2idx[0] += 1
                for j in range(4):
                    hc = g * 4 + j
                    for tl in range(4):
                        for dh in range(2):
                            k = tl * 2 + dh
                            extra = []
                            if hc == 0:
                                extra.append(prev_mm2ev[k])
                                if k < 2:
                                    extra.append(mm1ev.get(NHC - 2 + k))
                            mm2[k] = P.op("tensor",
                                          lambda e, hc=hc, tl=tl, dh=dh, k=k,
                                          gb2=gb2, j=j:
                                          e.matmul(
                                              PB[k][:],
                                              hqv[hc][:, tl * 128:(tl + 1) * 128],
                                              W2S[gb2][j][:, dh * 512:(dh + 1) * 512],
                                              start=(hc == 0),
                                              stop=(hc == NHC - 1)),
                                          waits=[hq_list[hc]] + l2 + extra)
                            mm2_of_gen[gg].append(mm2[k])
            for tl in range(4):
                b = tl % 2
                tt = c * 4 + tl
                lxr = P.op("sync", lambda e, b=b, tt=tt: e.dma_start(
                    out=xr[b][:], in_=src_dram.ap()[tt * 128:(tt + 1) * 128, :]),
                    waits=[store_ops[-2]] if len(store_ops) >= 2 else [],
                    dma=sl_xr[b])
                evs = []
                for dh in range(2):
                    k = tl * 2 + dh
                    ev2 = P.op("vector", lambda e, k=k, b=b, dh=dh, tl=tl:
                               e.scalar_tensor_tensor(
                                   xo[b][:, dh * 512:(dh + 1) * 512], PB[k][:],
                                   st2[:, 8 + tl:9 + tl],
                                   xr[b][:, dh * 512:(dh + 1) * 512],
                                   ALU.mult, ALU.add),
                               waits=[mm2[k], lxr, e3,
                                      store_ops[-2] if len(store_ops) >= 2 else None])
                    prev_mm2ev[k] = ev2
                    evs.append(ev2)
                store_ops.append(P.op("gpsimd", lambda e, b=b, tt=tt:
                                      e.dma_start(
                                          out=dst_dram.ap()[tt * 128:(tt + 1) * 128, :],
                                          in_=xo[b][:]), waits=evs, dma=sl_xo[b]))
            psxl[c % 2] = mm1ev[NHC - 1]
            prev_s2l = hq_list[-1]
        return store_ops

    def emit_attn(st, src_dram, dst_dram, tag="a"):
        SBL = lambda n, sh, dt_: SB(f"{tag}{n}", sh, dt_, st)
        Y = [SBL(f"y{d}", [128, T], BF16) for d in range(NDC)]
        Q = [SBL(f"q{d}", [128, T], BF16) for d in range(NDC)]
        K = [SBL(f"k{d}", [128, T], BF16) for d in range(NDC)]
        VE = [SBL(f"ve{t}", [128, 1040], BF16) for t in range(NT)]
        WB = [SBL(f"wb{d}", [128, D], BF16) for d in range(NDC)]
        RA = [SBL(f"ra{i}", [128, 1024], BF16) for i in range(2)]
        RB = [SBL(f"rb{i}", [128, 1024], BF16) for i in range(2)]
        PAb = [SBL(f"pa{i}", [128, 512], BF16) for i in range(2)]
        PBb = [SBL(f"pt{i}", [128, 512], BF16) for i in range(2)]
        rdr = SBL("rdr", [128, 1024], F32)
        bob = rdr
        xr = [SBL(f"xr{i}", [128, D], F32) for i in range(2)]
        xo = [SBL(f"xo{i}", [128, D], F32) for i in range(2)]
        t1b = SBL("t1b", [128, 512], F32)
        AC = Y

        sl_bo = P.slot(f"{tag}bo")
        sl_wl = P.slot(f"{tag}wl")
        sl_ra = [P.slot(f"{tag}ra{i}") for i in range(2)]
        sl_rb = [P.slot(f"{tag}rbs{i}") for i in range(2)]
        sl_rd1 = P.slot(f"{tag}rd1")
        sl_rd2 = P.slot(f"{tag}rd2")
        sl_rd3 = P.slot(f"{tag}rd3")
        sl_xr = [P.slot(f"{tag}xr{i}") for i in range(2)]
        sl_xo = [P.slot(f"{tag}xo{i}") for i in range(2)]
        ld_bo = None
        rz = [P.op("vector", lambda e, tt=tt: e.memset(tt[64:96, :], 0.0))
              for tt in (rdr, xr[0])]
        vz = []
        for t in range(NT):
            for h2 in range(16):
                vz.append(P.op("gpsimd", lambda e, t=t, h2=h2: e.memset(
                    VE[t][:, h2 * 65 + 64:h2 * 65 + 65], 1.0)))
        vz_last = vz[-1]

        y_done, _ = emit_ln(SBL, src_dram, False, None, Y, None, None, None,
                            xo[0], tag)

        def load_w(dsrc, waits):
            return [P.op("gpsimd", lambda e, dc=dc, dsrc=dsrc: e.dma_start(
                out=WB[dc][:], in_=dsrc.ap()[dc * 128:(dc + 1) * 128, :]),
                waits=waits, dma=sl_wl) for dc in range(NDC)]

        qk_ev = []
        lw = load_w(dwq, [])
        for dc in range(NDC):
            for qc in range(NCH):
                bank = (dc * NCH + qc) % 2
                mm = None
                for dn in range(NDC):
                    mm = P.op("tensor", lambda e, dn=dn, dc=dc, qc=qc, bank=bank:
                              e.matmul(PB[bank][:],
                                       WB[dn][:, dc * 128:(dc + 1) * 128],
                                       Y[dn][:, qc * 512:(qc + 1) * 512],
                                       start=(dn == 0), stop=(dn == NDC - 1)),
                              waits=lw + [y_done]
                              + ([qk_ev[-2]] if len(qk_ev) >= 2 else []))
                qk_ev.append(P.op("scalar", lambda e, dc=dc, qc=qc, bank=bank:
                                  e.activation(Q[dc][:, qc * 512:(qc + 1) * 512],
                                               PB[bank][:], AF.Identity,
                                               scale=S125[:]), waits=[mm]))
        lw2 = load_w(dwk, [qk_ev[-1]])
        kv_ev = []
        for dc in range(NDC):
            for qc in range(NCH):
                bank = (dc * NCH + qc) % 2
                mm = None
                for dn in range(NDC):
                    mm = P.op("tensor", lambda e, dn=dn, dc=dc, qc=qc, bank=bank:
                              e.matmul(PB[bank][:],
                                       WB[dn][:, dc * 128:(dc + 1) * 128],
                                       Y[dn][:, qc * 512:(qc + 1) * 512],
                                       start=(dn == 0), stop=(dn == NDC - 1)),
                              waits=lw2
                              + ([kv_ev[-2]] if len(kv_ev) >= 2 else []))
                kv_ev.append(P.op("scalar", lambda e, dc=dc, qc=qc, bank=bank:
                                  e.activation(K[dc][:, qc * 512:(qc + 1) * 512],
                                               PB[bank][:], AF.Copy),
                                  waits=[mm]))
        lw3 = load_w(dwv, [kv_ev[-1]])
        v_ev = []
        for t in range(NT):
            for dhf in range(2):
                bank = (t * 2 + dhf) % 2
                mm = None
                for dn in range(NDC):
                    mm = P.op("tensor", lambda e, dn=dn, t=t, dhf=dhf, bank=bank:
                              e.matmul(PB[bank][:],
                                       Y[dn][:, t * 128:(t + 1) * 128],
                                       WB[dn][:, dhf * 512:(dhf + 1) * 512],
                                       start=(dn == 0), stop=(dn == NDC - 1)),
                              waits=lw3
                              + ([v_ev[-9]] if len(v_ev) >= 16 else []))
                for hl in range(8):
                    h = dhf * 8 + hl
                    slot = (h // 2) * 130 + (h % 2) * 65
                    v_ev.append(P.op("vector",
                                     lambda e, t=t, hl=hl, slot=slot, bank=bank:
                                     e.tensor_copy(VE[t][:, slot:slot + 64],
                                                   PB[bank][:, hl * 64:(hl + 1) * 64]),
                                     waits=[mm, vz_last]))
        v_done = v_ev[-1]

        norm_ops = []
        bank_rd = {}
        pab_rd = {}
        ra_rd = {0: [], 1: []}
        pv_rd = {}
        d1_hist = {}
        rb_rd = {}
        last_rload = None
        prev_d1 = None
        for pr in range(8):
            rp = pr % 2
            rlA = P.op("gpsimd", lambda e, pr=pr, rp=rp: e.dma_start(
                out=RA[rp][:], in_=bass.AP(dDB, (2 * pr) * 2560 + 1408,
                                           [[1, 128], [1, 1024]])),
                waits=[db_sp, last_rload] + ra_rd[rp], dma=sl_ra[rp])
            rlB = P.op("gpsimd", lambda e, pr=pr, rp=rp: e.dma_start(
                out=RB[rp][:], in_=bass.AP(dDB, (2 * pr + 1) * 2560 + 1408,
                                           [[1, 128], [1, 1024]])),
                waits=[db_sp, last_rload] + ra_rd[rp], dma=sl_rb[rp])
            last_rload = rlB
            ra_rd[rp] = []
            for qc in range(NCH):
                nkt = 4 * qc + 4
                p = qc % 2
                pvA = (2, 0)[p]
                pvB = pvA + 1
                oa_mm = None
                ob_mm = None
                for kt in range(nkt):
                    kb = kt % 2
                    near = kt >= 4 * qc - 1
                    j0 = 2047 - 512 * qc + 128 * kt
                    bA = 4 + 2 * kb
                    bB = bA + 1
                    we = bank_rd.get(kb, (None, None))
                    mA = P.op("tensor", lambda e, pr=pr, qc=qc, kt=kt, bA=bA:
                              e.matmul(PB[bA][:],
                                       K[pr][0:64, kt * 128:(kt + 1) * 128],
                                       Q[pr][0:64, qc * 512:(qc + 1) * 512],
                                       start=True, stop=True,
                                       tile_position=(0, 0)),
                              waits=[qk_ev[-1], kv_ev[-1], we[0]])
                    mB = P.op("tensor", lambda e, pr=pr, qc=qc, kt=kt, bB=bB:
                              e.matmul(PB[bB][:],
                                       K[pr][64:128, kt * 128:(kt + 1) * 128],
                                       Q[pr][64:128, qc * 512:(qc + 1) * 512],
                                       start=True, stop=True,
                                       tile_position=(64, 0)),
                              waits=[we[1]])
                    wpab = pab_rd.get(kb, (None, None))
                    eA = P.op("scalar", lambda e, kb=kb, bA=bA: e.activation(
                        PAb[kb][:], PB[bA][:], AF.Exp, bias=BC12[:]),
                        waits=[mA, wpab[0]])
                    eB = P.op("scalar", lambda e, kb=kb, bB=bB: e.activation(
                        PBb[kb][:], PB[bB][:], AF.Exp, bias=BC12[:]),
                        waits=[mB, wpab[1]])
                    bank_rd[kb] = (eA, eB)
                    if near:
                        pA = P.op("vector", lambda e, kb=kb, j0=j0, rp=rp:
                                  e.tensor_mul(
                                      PAb[kb][:], PAb[kb][:],
                                      bass.AP(RA[rp], j0 - 1408, [[1024, 128], [-1, 512]])),
                                  waits=[eA, rlA])
                        pB = P.op("vector", lambda e, kb=kb, j0=j0, rp=rp:
                                  e.tensor_mul(
                                      PBb[kb][:], PBb[kb][:],
                                      bass.AP(RB[rp], j0 - 1408, [[1024, 128], [-1, 512]])),
                                  waits=[eB, rlB])
                        ra_rd[rp] += [pA, pB]
                        srcA, srcB = pA, pB
                    else:
                        srcA, srcB = eA, eB
                    slotA = pr * 130
                    oa_mm = P.op("tensor", lambda e, kt=kt, slotA=slotA,
                                 nkt=nkt, kb=kb, pvA=pvA:
                                 e.matmul(PB[pvA][0:65, :],
                                          VE[kt][:, slotA:slotA + 65],
                                          PAb[kb][:],
                                          start=(kt == 0), stop=(kt == nkt - 1)),
                                 waits=[srcA, v_done]
                                 + (pv_rd.get(pvA, []) if kt == 0 else []))
                    ob_mm = P.op("tensor", lambda e, kt=kt, slotA=slotA,
                                 nkt=nkt, kb=kb, pvB=pvB:
                                 e.matmul(PB[pvB][0:65, :],
                                          VE[kt][:, slotA + 65:slotA + 130],
                                          PBb[kb][:],
                                          start=(kt == 0), stop=(kt == nkt - 1)),
                                 waits=[srcB]
                                 + (pv_rd.get(pvB, []) if kt == 0 else []))
                    pab_rd[kb] = (oa_mm, ob_mm)
                sidx = pr * 4 + qc
                RCP = (rdr, xr[0])[p]
                rc1 = P.op("vector", lambda e, p=p, pvA=pvA, RCP=RCP:
                           e.reciprocal(
                               RCP[64:65, 0:512], PB[pvA][64:65, :]),
                           waits=[oa_mm] + d1_hist.get(p, []), small=True)
                rc2 = P.op("vector", lambda e, p=p, pvB=pvB, RCP=RCP:
                           e.reciprocal(
                               RCP[64:65, 512:1024], PB[pvB][64:65, :]),
                           waits=[ob_mm], small=True)
                shf = []
                for half, c0 in ((0, 0), (1, 512)):
                    for dq in range(2):
                        shf.append(P.op("vector",
                            lambda e, p=p, dq=dq, c0=c0, RCP=RCP:
                            e.stream_shuffle(
                                xo[p][dq * 32:(dq + 1) * 32, c0:c0 + 512],
                                RCP[64:96, c0:c0 + 512], [0] * 32),
                            waits=[rc1 if c0 == 0 else rc2]
                            + rb_rd.get(p, [])))
                d1_hist[p] = shf
                nA = P.op("vector", lambda e, pr=pr, qc=qc, p=p, pvA=pvA:
                          e.tensor_mul(
                              AC[pr][0:64, qc * 512:(qc + 1) * 512],
                              PB[pvA][0:64, :],
                              xo[p][0:64, 0:512]),
                          waits=shf[0:2] + [oa_mm, v_done])
                nB = P.op("vector", lambda e, pr=pr, qc=qc, p=p, pvB=pvB:
                          e.tensor_mul(
                              AC[pr][64:128, qc * 512:(qc + 1) * 512],
                              PB[pvB][0:64, :],
                              xo[p][0:64, 512:1024]),
                          waits=shf[2:4] + [ob_mm])
                pv_rd[pvA] = [rc1, nA]
                pv_rd[pvB] = [rc2, nB]
                rb_rd[p] = [nA, nB]
                norm_ops += [nA, nB]
        ac_done = norm_ops[-1]

        ld_bo = P.op("gpsimd", lambda e: e.dma_start(
            out=bob[:], in_=bass.AP(dbo, 0, [[0, 128], [1, D]])),
            waits=[ac_done] + d1_hist.get(0, []) + d1_hist.get(1, []),
            dma=sl_bo)
        lw4 = load_w(dwo, [ac_done])
        store_ops = []
        wo_ev = []
        for t in range(NT):
            b = t % 2
            lxr = P.op("sync", lambda e, b=b, t=t: e.dma_start(
                out=xr[b][:], in_=src_dram.ap()[t * 128:(t + 1) * 128, :]),
                waits=([store_ops[-2]] if len(store_ops) >= 2 else
                       d1_hist.get(0, []) + d1_hist.get(1, [])),
                dma=sl_xr[b])
            evs = []
            for dhf in range(2):
                bank = (t * 2 + dhf) % 2
                mm = None
                for dn in range(NDC):
                    mm = P.op("tensor", lambda e, dn=dn, t=t, dhf=dhf, bank=bank:
                              e.matmul(PB[bank][:],
                                       AC[dn][:, t * 128:(t + 1) * 128],
                                       WB[dn][:, dhf * 512:(dhf + 1) * 512],
                                       start=(dn == 0), stop=(dn == NDC - 1)),
                              waits=lw4 + [ac_done]
                              + ([wo_ev[-2]] if len(wo_ev) >= 2 else []))
                tb = P.op("vector", lambda e, bank=bank, dhf=dhf: e.tensor_add(
                    t1b[:], PB[bank][:], bob[:, dhf * 512:(dhf + 1) * 512]),
                    waits=[mm, ld_bo])
                ev = P.op("vector", lambda e, b=b, dhf=dhf: e.tensor_add(
                    xo[b][:, dhf * 512:(dhf + 1) * 512], t1b[:],
                    xr[b][:, dhf * 512:(dhf + 1) * 512]),
                    waits=[tb, lxr,
                           store_ops[-2] if len(store_ops) >= 2 else None])
                wo_ev.append(ev)
                evs.append(ev)
            store_ops.append(P.op("gpsimd", lambda e, b=b, t=t: e.dma_start(
                out=dst_dram.ap()[t * 128:(t + 1) * 128, :], in_=xo[b][:]),
                waits=evs, dma=sl_xo[b]))
        return store_ops

    if NPHASES == 1:
        cp = [P.op("gpsimd", lambda e, t=t: e.dma_start(
            out=dout.ap()[t * 128:(t + 1) * 128, :],
            in_=dx.ap()[t * 128:(t + 1) * 128, :]),
            waits=[db_sp] + sp1[-2:] + sp2[-2:], dma=P.slot("cp") if t == 0 else P.slots[-1])
            for t in range(NT)]
        c_stores = cp
    elif NPHASES == 2:
        st1 = ExitStack()
        c_stores = emit_ffn(st1, dx, dout, "fa")
        st1.close()
    elif NPHASES == 3:
        st1 = ExitStack()
        emit_ffn(st1, dx, dX1, "fa")
        P.barrier()
        st1.close()
        st2x = ExitStack()
        c_stores = emit_attn(st2x, dX1, dout)
        st2x.close()
    else:
        st1 = ExitStack()
        emit_ffn(st1, dx, dX1, "fa")
        P.barrier()
        st1.close()

        st2x = ExitStack()
        emit_attn(st2x, dX1, dX2)
        P.barrier()
        st2x.close()

        st3 = ExitStack()
        c_stores = emit_ffn(st3, dX2, dout, "fc")
        st3.close()

    if DEBUG:
        sl_dbg = P.slot("dbg")
        dbgsx = nc.dram_tensor("dbg_sx", [2052], F32, kind="ExternalOutput")
        dbgs2 = nc.dram_tensor("dbg_s2", [T], F32, kind="ExternalOutput")
        dbgw1 = nc.dram_tensor("dbg_w1", [128, HFF], BF16, kind="ExternalOutput")
        dbgdb = nc.dram_tensor("dbg_db", [HEADS, 2560], BF16, kind="ExternalOutput")
        last = P.last
        ws = [o for o in last.values() if o is not None]
        P.op("gpsimd", lambda e: e.dma_start(out=dbgsx.ap()[0:2052],
                                             in_=dSX.ap()[0:2052]),
             waits=ws, dma=sl_dbg)
        P.op("gpsimd", lambda e: e.dma_start(out=dbgs2.ap()[0:T],
                                             in_=dS2.ap()[0:T]), dma=sl_dbg)
        P.op("gpsimd", lambda e: e.dma_start(out=dbgw1.ap()[:, :],
                                             in_=dWQ1.ap()[0, :, :]), dma=sl_dbg)
        P.op("gpsimd", lambda e: e.dma_start(out=dbgdb.ap()[:, :],
                                             in_=dDB.ap()[:, :]), dma=sl_dbg)
        P.op("gpsimd", None, waits=[P.last["gpsimd"]])

    P.op("sync", None, waits=c_stores)

    counts = P.emit(stack)
    return nc, stack, counts


_CACHE = {}


def kernel(**ins):
    xfull = np.ascontiguousarray(np.asarray(ins["L0_attn_x"], dtype=np.float32))
    w1t = np.ascontiguousarray(np.asarray(ins["L0_ffn_w1"], np.float32).T)
    w2t = np.ascontiguousarray(np.asarray(ins["L0_ffn_w2"], np.float32).T)
    cw = np.ascontiguousarray(
        np.asarray(ins["L0_ffn_conv_w"], np.float32)[:, 0, :]
        .reshape(NHC, 128, 3).transpose(2, 1, 0))
    cb = np.ascontiguousarray(
        np.asarray(ins["L0_ffn_conv_b"], np.float32).reshape(NHC, 128).T)
    common = {
        "wq": np.ascontiguousarray(np.asarray(ins["L0_attn_Wq"], np.float32)),
        "wk": np.ascontiguousarray(np.asarray(ins["L0_attn_Wk"], np.float32)),
        "wv": np.ascontiguousarray(np.asarray(ins["L0_attn_Wv"], np.float32)),
        "wo": np.ascontiguousarray(np.asarray(ins["L0_attn_Wo"], np.float32)),
        "bo": np.ascontiguousarray(np.asarray(ins["L0_attn_bo"], np.float32)),
        "rel": np.ascontiguousarray(np.asarray(ins["L0_attn_rel_emb"], np.float32)),
        "w1t": w1t, "w2t": w2t, "cw": cw, "cb": cb,
    }
    if "nc" not in _CACHE:
        nc, stck, counts = build_program()
        _CACHE["nc"] = nc
        _CACHE["stack"] = stck
        print("bass counts:", counts, "sbuf_free:", nc.sbuf_bytes_remaining)
    nc = _CACHE["nc"]
    in_maps = [dict(common, x=np.ascontiguousarray(xfull[i])) for i in range(8)]
    res = run_bass_kernel_spmd(nc, in_maps, core_ids=list(range(8)))
    return np.stack([res.results[i]["out"] for i in range(8)], axis=0)
